# version 12
# baseline (speedup 1.0000x reference)
import numpy as np
import ml_dtypes
from contextlib import ExitStack

import concourse.bass as bass
import concourse.mybir as mybir
from concourse.bass_utils import run_bass_kernel_spmd
from concourse.aot_env import allow_ici_all_to_all

F32 = mybir.dt.float32
BF16 = mybir.dt.bfloat16
AF = mybir.ActivationFunctionType
ALU = mybir.AluOpType
AX = mybir.AxisListType

NC = 8
B, T, D = 2, 4096, 2048
H, HD = 16, 128
IC = D // 128
FF = 8192
TOKL = 1024
ROPE_BASE = 10000.0
S128 = float(np.float32(1.0) / np.sqrt(np.float32(128.0)))
XT = 256
NXB = (B * T) // XT

ENGS = ("sync", "gpsimd", "scalar", "vector", "tensor")


class _Sem:
    __slots__ = ("name", "n", "h")

    def __init__(self, name):
        self.name = name
        self.n = 0
        self.h = None


class _Sched:
    def __init__(self):
        self.ops = {e: [] for e in ENGS}
        self.sems = []

    def sem(self, name):
        s = _Sem(name)
        self.sems.append(s)
        return s

    def wait(self, eng, s, val=None):
        v = s.n if val is None else val
        if v > 0:
            self.ops[eng].append(("w", s, v))

    def emit(self, eng, fn, inc=None, k=1):
        self.ops[eng].append(("i", fn, inc, k))
        if inc is not None:
            inc.n += k
            return (inc, inc.n)
        return None


def build_nc(debug_taps=False):
    nc = bass.Bass("TRN2", target_bir_lowering=False)

    dr = lambda n, s, d, k: nc.dram_tensor(n, s, d, kind=k).ap()
    x_fm = dr("x_fm", [128, IC, B * T], F32, "ExternalInput")
    x_loc = dr("x_loc", [128, IC, TOKL], F32, "ExternalInput")
    wq_i = dr("wq_i", [128, 4096], F32, "ExternalInput")
    wk_i = dr("wk_i", [128, 4096], F32, "ExternalInput")
    wv_i = dr("wv_i", [128, 4096], F32, "ExternalInput")
    wp_i = dr("wp_i", [128, IC, 2048], F32, "ExternalInput")
    w1_i = dr("w1_i", [128, 2, 32, 2048], F32, "ExternalInput")
    w2_i = dr("w2_i", [128, 2, IC, 2, 2048], F32, "ExternalInput")
    b1s_i = dr("b1s_i", [128, 64], F32, "ExternalInput")
    b2s_i = dr("b2s_i", [128, 16], F32, "ExternalInput")
    qc_i = dr("qc_i", [128, T], BF16, "ExternalInput")
    qs_i = dr("qs_i", [128, T], BF16, "ExternalInput")
    kc_i = dr("kc_i", [128, T], BF16, "ExternalInput")
    ks_i = dr("ks_i", [128, T], BF16, "ExternalInput")
    mask_i = dr("mask_i", [128, 4 * 512], BF16, "ExternalInput")
    perm_i = dr("perm_i", [128, 128], BF16, "ExternalInput")
    idn_i = dr("idn_i", [128, 128], BF16, "ExternalInput")
    onesc_i = dr("onesc_i", [1, 128], BF16, "ExternalInput")
    onesk_i = dr("onesk_i", [128, 1], BF16, "ExternalInput")
    onesc32_i = dr("onesc32_i", [1, 128], F32, "ExternalInput")
    onesk32_i = dr("onesk32_i", [128, 1], F32, "ExternalInput")
    out_fm = dr("out_fm", [D, TOKL], F32, "ExternalOutput")

    a2a_in = [dr(f"a2a_in{hl}", [NC * 128, TOKL], BF16, "Internal")
              for hl in range(2)]
    a2a_out = [dr(f"a2a_out{hl}", [NC * 128, TOKL], BF16, "Internal")
               for hl in range(2)]
    if debug_taps:
        dbg_h = dr("dbg_h", [128, 32768], BF16, "Internal")
        dbg_acc = dr("dbg_acc", [128, 16384], F32, "Internal")

    sc = _Sched()
    LD = sc.sem("LD")
    LS = [sc.sem("LS0"), sc.sem("LS1")]
    LX = sc.sem("LX")
    GX = [sc.sem("GX0"), sc.sem("GX1")]
    GY = [sc.sem("GY0"), sc.sem("GY1")]
    GW = sc.sem("GW")
    GP = [sc.sem("GP0"), sc.sem("GP1")]
    GF = sc.sem("GF")
    GQ = sc.sem("GQ")
    CC = sc.sem("CC")
    CC2 = sc.sem("CC2")
    PE = sc.sem("PE")
    AC = sc.sem("AC")
    DV = sc.sem("DV")
    ST = sc.sem("ST")

    with ExitStack() as ctx:
        sb = lambda n, s, d: ctx.enter_context(nc.sbuf_tensor(n, s, d))
        R0 = sb("R0", [128, 8192], BF16)
        R1 = sb("R1", [128, 4096], F32)
        R2 = sb("R2", [128, 16384], F32)
        R3 = sb("R3", [128, 49152], BF16)
        MK = sb("MK", [128, 2048], BF16)
        ROWS = sb("ROWS", [1, 2048], F32)
        PERM = sb("PERM", [128, 128], BF16)
        IDN = sb("IDN", [128, 128], BF16)
        ONESC = sb("ONESC", [1, 128], BF16)
        ONESK = sb("ONESK", [128, 1], BF16)
        ONESC32 = sb("ONESC32", [1, 128], F32)
        ONESK32 = sb("ONESK32", [128, 1], F32)
        B1S = sb("B1S", [128, 64], F32)
        B2S = sb("B2S", [128, 16], F32)
        THR = sb("THR", [128, 4], F32)
        WSUM = sb("WSUM", [128, 516], F32)
        pb = [ctx.enter_context(nc.psum_tensor(f"pb{i}", [128, 512], F32))
              for i in range(8)]
        for s in sc.sems:
            s.h = ctx.enter_context(nc.semaphore(s.name))
        block = ctx.enter_context(nc.Block())

        R2b = R2[:].bitcast(BF16)

        QCT = R2b[:, 0:4096]
        QST = R2b[:, 4096:8192]
        KCT = R2b[:, 8192:12288]
        KST = R2b[:, 12288:16384]
        XBLK = [R2b[:, 16384:20480], R2b[:, 20480:24576]]
        WVBF = R2b[:, 24576:28672]
        RAWQ = R2b[:, 28672:28928]
        RAWK = R2b[:, 28928:29184]
        TSIN = R2[:, 14848:15104]
        TCOS = R2[:, 15104:15360]
        RLB = [R2[:, 1024:1536], R2[:, 1536:2048]]
        WQBF = R0[:, 0:4096]
        WKBF = R0[:, 4096:8192]
        PBUF6 = [R0[:, i * 512:(i + 1) * 512] for i in range(6)]
        YOUT = [R0[:, 3072 + i * 512: 3584 + i * 512] for i in range(2)]
        WPBF = [R0[:, i * 2048:(i + 1) * 2048] for i in range(2)]
        SGA2 = [R0[:, 0:2048], R0[:, 2048:4096]]
        WQT = [R0[:, 4096:6144], R0[:, 6144:8192]]
        WS32 = [R1[:, 0:2048], R1[:, 2048:4096]]
        MB_T = [R1[:, 0:512], R1[:, 1024:1536]]
        RB_T = [R1[:, 512:1024], R1[:, 1536:2048]]
        qb = R3[:, 0:16384]
        kb = R3[:, 16384:32768]
        vb = R3[:, 32768:49152]
        y_fm = R3[:, 0:16384]
        rbf = R3[:, 16384:32768]
        x1b = R3[:, 32768:49152]
        sqb = R3[:, 0:16384]
        h_all = R3[:, 0:32768]
        rbf2 = R3[:, 0:16384]
        sqb2 = R3[:, 16384:32768]
        RLROW = [ROWS[:, 0:512], ROWS[:, 512:1024]]
        LNROW = [[ROWS[:, 512 + k * 512: 1024 + k * 512] for k in range(3)],
                 [R1[0:1, 2048 + k * 512: 2560 + k * 512] for k in range(3)]]
        TINY = ROWS[:, 0:1]

        m_consts = None
        for (dst, src) in [
            (QCT, qc_i), (QST, qs_i), (KCT, kc_i), (KST, ks_i),
            (MK[:], mask_i), (PERM[:], perm_i), (IDN[:], idn_i),
            (ONESC[:], onesc_i),
            (ONESK[:], onesk_i), (ONESC32[:], onesc32_i),
            (ONESK32[:], onesk32_i), (B1S[:], b1s_i), (B2S[:], b2s_i),
        ]:
            m_consts = sc.emit("sync", lambda e, d=dst, s=src: e.dma_start(out=d, in_=s), LD, 16)
        sc.emit("gpsimd", lambda e: e.dma_start(out=WQBF, in_=wq_i), GW, 16)
        sc.emit("gpsimd", lambda e: e.dma_start(out=WKBF, in_=wk_i), GW, 16)
        m_wqkv = sc.emit("gpsimd", lambda e: e.dma_start(out=WVBF, in_=wv_i), GW, 16)

        scale_blocks = ([("w1", h, c) for h in range(2) for c in range(32)]
                        + [("w2", h, o, s_) for h in range(2) for o in range(IC)
                           for s_ in range(2)])

        def blk_col(blk):
            if blk[0] == "w1":
                return blk[1] * 32 + blk[2]
            _, half, oc, hs = blk
            return 64 + (half * IC + oc) * 2 + hs

        sblk = {"i": 0}
        wstream_free = [None, None]

        def emit_scale_blocks(kk):
            for _ in range(kk):
                i = sblk["i"]
                if i >= len(scale_blocks):
                    return
                blk = scale_blocks[i]
                bi = i % 2
                wsl = WS32[bi]
                if wstream_free[bi] is not None:
                    sc.wait("sync", *wstream_free[bi])
                if blk[0] == "w1":
                    _, half, hc = blk
                    src = w1_i[:, half, hc, :]
                else:
                    _, half, oc, hs = blk
                    src = w2_i[:, half, oc, hs, :]
                for ph in range(2):
                    m_d = sc.emit("sync", lambda e, dst=wsl, s=src, ph=ph: e.dma_start(
                        out=dst[:, ph * 1024:(ph + 1) * 1024],
                        in_=s[:, ph * 1024:(ph + 1) * 1024]), LS[bi], 16)
                sc.wait("vector", LS[bi], m_d[1])
                col = 4 * blk_col(blk)
                for ch in range(4):
                    sc.emit("vector", lambda e, wsl=wsl, c=col + ch, ch=ch: e.tensor_reduce(
                        out=WSUM[:, c:c + 1], in_=wsl[:, ch * 512:(ch + 1) * 512],
                        axis=AX.X, op=ALU.add, apply_absolute_value=True), DV)
                wstream_free[bi] = (DV, DV.n)
                sblk["i"] += 1

        m_xblk = [None] * NXB
        xblk_free = [None, None]
        sc.wait("tensor", GW, m_wqkv[1])
        sc.wait("tensor", LD, m_consts[1])

        m_rawfree = {"q": None, "k": None}
        m_qkps_free = {"q": None, "k": None}
        m_permfree = [None, None]
        m_vps_free = [None] * 4
        last_rope_dve = None

        for tb in range(NXB):
            bi = tb % 2
            b = tb // 16
            pos0 = (tb % 16) * XT
            if xblk_free[bi] is not None:
                sc.wait("gpsimd", *xblk_free[bi])
            m_xblk[tb] = sc.emit("gpsimd", lambda e, bi=bi, tb=tb: e.dma_start(
                out=XBLK[bi].rearrange("p (ic t) -> p ic t", ic=IC),
                in_=x_fm[:, :, tb * XT:(tb + 1) * XT]),
                GX[bi], 16)
            sc.wait("tensor", GX[bi], m_xblk[tb][1])
            for hl in range(2):
                bh = hl * 2 + b
                for which, wbf, psl, raw, ctab, stab, dstq in (
                    ("q", WQBF, pb[0], RAWQ, QCT, QST, qb),
                    ("k", WKBF, pb[1], RAWK, KCT, KST, kb),
                ):
                    if m_qkps_free[which] is not None:
                        sc.wait("tensor", *m_qkps_free[which])
                    m_mm = None
                    for ic in range(IC):
                        r = sc.emit("tensor", lambda e, w=wbf, ic=ic, hl=hl, bi=bi, ps=psl: e.matmul(
                            ps[:, 0:XT],
                            w[:, (ic * 2 + hl) * 128:(ic * 2 + hl + 1) * 128],
                            XBLK[bi][:, ic * XT:(ic + 1) * XT],
                            start=(ic == 0), stop=(ic == IC - 1)),
                            PE if ic == IC - 1 else None)
                        if r is not None:
                            m_mm = r
                    if m_rawfree[which] is not None:
                        sc.wait("scalar", *m_rawfree[which])
                    sc.wait("scalar", PE, m_mm[1])
                    m_raw = sc.emit("scalar", lambda e, ps=psl, raw=raw: e.activation(
                        out=raw, in_=ps[:, 0:XT], func=AF.Copy), AC)
                    m_qkps_free[which] = (AC, m_raw[1])
                    if m_permfree[hl] is not None:
                        sc.wait("tensor", *m_permfree[hl])
                    sc.wait("tensor", AC, m_raw[1])
                    m_sw = sc.emit("tensor", lambda e, raw=raw, hl=hl: e.matmul(
                        pb[2 + hl][:, 0:XT], PERM[:], raw, start=True, stop=True), PE)
                    sc.wait("vector", PE, m_sw[1])
                    if last_rope_dve is not None:
                        sc.wait("vector", DV, last_rope_dve[1])
                    sc.emit("vector", lambda e, hl=hl, st=stab, pos0=pos0: e.tensor_tensor(
                        out=TSIN, in0=pb[2 + hl][:, 0:XT], in1=st[:, pos0:pos0 + XT], op=ALU.mult), DV)
                    m_tc = sc.emit("vector", lambda e, raw=raw, ct=ctab, pos0=pos0: e.tensor_tensor(
                        out=TCOS, in0=raw, in1=ct[:, pos0:pos0 + XT], op=ALU.mult), DV)
                    sc.wait("vector", DV, m_tc[1])
                    m_rot = sc.emit("vector", lambda e, dq=dstq, bh=bh, pos0=pos0: e.tensor_tensor(
                        out=dq[:, bh * 4096 + pos0: bh * 4096 + pos0 + XT],
                        in0=TSIN, in1=TCOS, op=ALU.add), DV)
                    last_rope_dve = m_rot
                    m_rawfree[which] = (DV, m_rot[1])
                    m_permfree[hl] = (DV, m_rot[1])
                if hl == 0:
                    for s_ in range(2):
                        vslot = (tb * 2 + s_) % 4
                        psv = pb[4 + vslot]
                        if m_vps_free[vslot] is not None:
                            sc.wait("tensor", *m_vps_free[vslot])
                        m_mv = None
                        for ic in range(IC):
                            r = sc.emit("tensor", lambda e, s_=s_, ic=ic, bi=bi, psv=psv: e.matmul(
                                psv[:, 0:256],
                                XBLK[bi][:, ic * XT + s_ * 128: ic * XT + s_ * 128 + 128],
                                WVBF[:, ic * 256:(ic + 1) * 256],
                                start=(ic == 0), stop=(ic == IC - 1)),
                                PE if ic == IC - 1 else None)
                            if r is not None:
                                m_mv = r
                        sc.wait("scalar", PE, m_mv[1])
                        jsub = (tb % 16) * 2 + s_
                        m_ve = None
                        for hl2 in range(2):
                            bh2 = hl2 * 2 + b
                            m_ve = sc.emit("scalar", lambda e, psv=psv, bh2=bh2, jsub=jsub, hl2=hl2: e.activation(
                                out=vb[:, bh2 * 4096 + jsub * 128: bh2 * 4096 + jsub * 128 + 128],
                                in_=psv[:, hl2 * 128:(hl2 + 1) * 128], func=AF.Copy), AC)
                        m_vps_free[vslot] = (AC, m_ve[1])
            xblk_free[bi] = (PE, PE.n)
            emit_scale_blocks(2)

        sc.wait("tensor", DV, DV.n)
        sc.wait("tensor", AC, AC.n)

        LOOKA = 3
        exp_done = [None] * 4
        slot_free = [None] * 6
        yp_free = [None, None]
        rl_free = [None, None]
        rlb_free = [None, None]
        yout_free = [None, None]
        rl_inuse = [None, None]
        pb7_free = [None]
        pend = []
        fin_pend = []

        y_fm_hv = y_fm.rearrange("p (s two t) -> p s two t", s=8, two=2)

        def emit_half_exchange(hl):
            sc.wait("gpsimd", GY[0], GY[0].n)
            sc.wait("gpsimd", GY[1], GY[1].n)

            def _cc(e, hl=hl):
                with allow_ici_all_to_all():
                    return e.collective_compute(
                        "AllToAll", ALU.bypass,
                        replica_groups=[list(range(NC))],
                        ins=[a2a_in[hl].opt()], outs=[a2a_out[hl].opt()])

            sc.emit("gpsimd", _cc, CC if hl == 0 else CC2)

        def fin_stage(f):
            par = f["par"]
            st = f["stage"]
            if st == 0:
                rl = RLROW[par]
                sc.wait("vector", PE, f["lstop"])
                if rl_inuse[par] is not None:
                    sc.wait("vector", *rl_inuse[par])
                f["m_r"] = sc.emit("vector", lambda e, par=par, rl=rl: e.reciprocal(
                    out=rl, in_=pb[6][64 * par:64 * par + 1, :]), DV)
                rl_free[par] = (DV, f["m_r"][1])
            elif st == 1:
                rl = RLROW[par]
                if pb7_free[0] is not None:
                    sc.wait("tensor", *pb7_free[0])
                sc.wait("tensor", DV, f["m_r"][1])
                f["m_bc"] = sc.emit("tensor", lambda e, rl=rl: e.matmul(
                    pb[7][:], ONESC32[:], rl, start=True, stop=True), PE)
                rl_inuse[par] = (PE, f["m_bc"][1])
            elif st == 2:
                if rlb_free[par] is not None:
                    sc.wait("scalar", *rlb_free[par])
                sc.wait("scalar", PE, f["m_bc"][1])
                f["m_cp"] = sc.emit("scalar", lambda e, par=par: e.activation(
                    out=RLB[par], in_=pb[7][:], func=AF.Copy), AC)
                pb7_free[0] = (AC, f["m_cp"][1])
            else:
                if yout_free[par] is not None:
                    sc.wait("vector", *yout_free[par])
                sc.wait("vector", AC, f["m_cp"][1])
                m_yo = sc.emit("vector", lambda e, par=par, ypb=f["ypb"]: e.tensor_tensor(
                    out=YOUT[par], in0=ypb[:], in1=RLB[par], op=ALU.mult), DV)
                yp_free[par] = (DV, m_yo[1])
                rlb_free[par] = (DV, m_yo[1])
                sc.wait("sync", DV, m_yo[1])
                r0 = (f["b"] * 4 + f["qtb"] // 2) * 128
                c0 = (f["qtb"] % 2) * 512
                m_st = sc.emit("sync", lambda e, r0=r0, c0=c0, par=par, hl=f["hl"]: e.dma_start(
                    out=a2a_in[hl][r0:r0 + 128, c0:c0 + 512], in_=YOUT[par]), GY[par], 16)
                yout_free[par] = (GY[par], m_st[1])
                if f["hl"] == 0 and f["b"] == 1 and f["qtb"] == 7:
                    emit_half_exchange(0)
                emit_scale_blocks(4)
            f["stage"] += 1

        def pop_one():
            r = pend.pop(0)
            while fin_pend:
                f0 = fin_pend[0]
                if f0["stage"] < 4 and f0["age"] >= f0["stage"] + 1:
                    fin_stage(f0)
                    if f0["stage"] == 4:
                        fin_pend.pop(0)
                        continue
                break
            for f in fin_pend:
                f["age"] += 1
            par = r["par"]
            sc.wait("tensor", AC, r["expv"])
            if r["j"] == 0:
                if yp_free[par] is not None:
                    sc.wait("tensor", *yp_free[par])
                if rl_free[par] is not None:
                    sc.wait("tensor", *rl_free[par])
            sc.emit("tensor", lambda e, r=r: e.matmul(
                r["ypb"][:],
                vb[:, r["bh"] * 4096 + r["j"] * 128: r["bh"] * 4096 + (r["j"] + 1) * 128],
                PBUF6[r["sl"]], start=(r["j"] == 0), stop=(r["j"] == r["n"] - 1)), None)
            m_l = sc.emit("tensor", lambda e, r=r: e.matmul(
                pb[6][64 * r["par"]:64 * r["par"] + 1, :], ONESK[:], PBUF6[r["sl"]],
                start=(r["j"] == 0), stop=(r["j"] == r["n"] - 1)), PE)
            slot_free[r["sl"]] = (PE, m_l[1])
            if r["j"] == r["n"] - 1:
                fin_pend.append(dict(age=0, stage=0, par=par, ypb=r["ypb"],
                                     lstop=m_l[1], hl=r["hl"], b=r["b"],
                                     qtb=r["qtb"]))

        gj = 0
        for (hl, b) in [(0, 0), (0, 1), (1, 0), (1, 1)]:
            bh = hl * 2 + b
            for qtb in range(8):
                n = 4 * qtb + 4
                par = qtb % 2
                ypb = pb[4 + par]
                for j in range(n):
                    sb_i = gj % 4
                    sl_i = gj % 6
                    sp = pb[sb_i]
                    diag = j >= n - 4
                    if exp_done[sb_i] is not None:
                        sc.wait("tensor", *exp_done[sb_i])
                    m_s = sc.emit("tensor", lambda e, sp=sp, bh=bh, j=j, qtb=qtb, diag=diag: e.matmul(
                        sp[:], kb[:, bh * 4096 + j * 128: bh * 4096 + (j + 1) * 128],
                        qb[:, bh * 4096 + qtb * 512: bh * 4096 + (qtb + 1) * 512],
                        start=True, stop=(not diag)), None if diag else PE)
                    if diag:
                        p_ = j - (n - 4)
                        m_s = sc.emit("tensor", lambda e, sp=sp, p_=p_: e.matmul(
                            sp[:], IDN[:], MK[:, p_ * 512:(p_ + 1) * 512],
                            start=False, stop=True), PE)
                    if slot_free[sl_i] is not None:
                        sc.wait("scalar", *slot_free[sl_i])
                    sc.wait("scalar", PE, m_s[1])
                    m_e = sc.emit("scalar", lambda e, sp=sp, sl_i=sl_i: e.activation(
                        out=PBUF6[sl_i], in_=sp[:], func=AF.Exp, scale=S128), AC)
                    exp_done[sb_i] = (AC, m_e[1])
                    pend.append(dict(sl=sl_i, expv=m_e[1], par=par, bh=bh,
                                     j=j, n=n, ypb=ypb, hl=hl, b=b, qtb=qtb))
                    gj += 1
                    if len(pend) > LOOKA:
                        pop_one()
        while pend:
            pop_one()
        while fin_pend:
            f0 = fin_pend.pop(0)
            while f0["stage"] < 4:
                fin_stage(f0)
        emit_scale_blocks(len(scale_blocks))

        emit_half_exchange(1)
        sc.wait("gpsimd", CC, 1)
        sc.wait("gpsimd", CC2, 1)
        m_yfm = None
        for hl2 in range(2):
            m_yfm = sc.emit("gpsimd", lambda e, hl2=hl2: e.dma_start(
                out=y_fm_hv[:, :, hl2, :],
                in_=a2a_out[hl2].rearrange("(s p) t -> p s t", p=128)), GF, 16)

        sc.wait("vector", DV, DV.n)
        sc.wait("vector", GQ, GQ.n)
        sc.emit("vector", lambda e: e.tensor_reduce(
            out=WSUM[:, 512:513], in_=WSUM[:, 0:256], axis=AX.X, op=ALU.add), DV)
        m_t2 = sc.emit("vector", lambda e: e.tensor_reduce(
            out=WSUM[:, 513:514], in_=WSUM[:, 256:512], axis=AX.X, op=ALU.add), DV)
        sc.wait("tensor", DV, m_t2[1])
        m_cp2 = None
        for (csrc, cth) in [(512, 0), (513, 2)]:
            if m_cp2 is not None:
                sc.wait("tensor", AC, m_cp2[1])
            m_mm1 = sc.emit("tensor", lambda e, cs=csrc: e.matmul(
                pb[7][0:1, 0:1], ONESK32[:], WSUM[:, cs:cs + 1], start=True, stop=True), PE)
            sc.wait("scalar", PE, m_mm1[1])
            sc.wait("scalar", AC, AC.n)
            m_cp = sc.emit("scalar", lambda e: e.activation(
                out=TINY, in_=pb[7][0:1, 0:1], func=AF.Copy,
                scale=0.5 / (FF * D)), AC)
            sc.wait("tensor", AC, m_cp[1])
            m_mm2 = sc.emit("tensor", lambda e: e.matmul(
                pb[7][:, 0:1], ONESC32[:], TINY, start=True, stop=True), PE)
            sc.wait("scalar", PE, m_mm2[1])
            m_cp2 = sc.emit("scalar", lambda e, ct=cth: e.activation(
                out=THR[:, ct + 1:ct + 2], in_=pb[7][:, 0:1], func=AF.Copy), AC)
            sc.wait("vector", AC, m_cp2[1])
            sc.emit("vector", lambda e, ct=cth: e.tensor_scalar(
                out=THR[:, ct:ct + 1], in0=THR[:, ct + 1:ct + 2],
                scalar1=-1.0, scalar2=None, op0=ALU.mult), DV)

        sc.wait("sync", DV, DV.n)
        m_xloc = sc.emit("sync", lambda e: e.dma_start(
            out=R2[:].rearrange("p (a t) -> p a t", a=IC),
            in_=x_loc), LX, 16)
        m_wp = [None] * IC
        sc.wait("gpsimd", DV, DV.n)

        def emit_wp(oc):
            bi = oc % 2
            m_wp[oc] = sc.emit("gpsimd", lambda e, oc=oc, bi=bi: e.dma_start(
                out=WPBF[bi], in_=wp_i[:, oc, :]), GP[bi], 16)

        emit_wp(0)
        emit_wp(1)
        sc.wait("tensor", GF, m_yfm[1])
        sc.wait("vector", LX, m_xloc[1])
        m_prev_evac = [None, None]
        for oc in range(IC):
            bi = oc % 2
            sc.wait("tensor", GP[bi], m_wp[oc][1])
            for th in range(2):
                if m_prev_evac[th] is not None:
                    sc.wait("tensor", *m_prev_evac[th])
            m_p = [None, None]
            for ic in range(IC):
                for th in range(2):
                    r = sc.emit("tensor", lambda e, bi=bi, ic=ic, th=th: e.matmul(
                        pb[th][:], WPBF[bi][:, ic * 128:(ic + 1) * 128],
                        y_fm[:, ic * 1024 + th * 512: ic * 1024 + th * 512 + 512],
                        start=(ic == 0), stop=(ic == IC - 1)),
                        PE if ic == IC - 1 else None)
                    if r is not None:
                        m_p[th] = r
            for th in range(2):
                sc.wait("vector", PE, m_p[th][1])
                col = oc * 1024 + th * 512
                m_r1 = sc.emit("vector", lambda e, th=th, col=col: e.tensor_tensor(
                    out=R2[:, col:col + 512], in0=pb[th][:], in1=R2[:, col:col + 512],
                    op=ALU.add), DV)
                m_prev_evac[th] = (DV, m_r1[1])
            if oc + 2 < IC:
                sc.wait("gpsimd", PE, PE.n)
                emit_wp(oc + 2)

        LN_SUMBK = ((2, 3), (6, 7))
        LN_BCBK = ((4, 5), (0, 1))

        out_fm_v = out_fm.rearrange("(a p) t -> p a t", p=128)

        def layer_norm(cast_b, sq_b, out_bf, store_out=False):
            sc.wait("scalar", DV, DV.n)
            sc.wait("scalar", AC, AC.n)
            sc.wait("scalar", PE, PE.n)
            sc.wait("tensor", AC, AC.n)
            sc.wait("tensor", DV, DV.n)
            R2v = R2[:].rearrange("p (a t) -> p a t", a=IC)
            cbv = cast_b.rearrange("p (a t) -> p a t", a=IC)
            sqv = sq_b.rearrange("p (a t) -> p a t", a=IC)
            m_sq = {}
            for th in range(2):
                sl = slice(th * 512, th * 512 + 512)
                sc.emit("scalar", lambda e, sl=sl: e.activation(
                    out=cbv[:, :, sl], in_=R2v[:, :, sl], func=AF.Copy), AC)
                m_sq[th] = sc.emit("scalar", lambda e, sl=sl: e.activation(
                    out=sqv[:, :, sl], in_=R2v[:, :, sl], func=AF.Square), AC)
            m_sums = {}
            for th in range(2):
                b1, b2 = LN_SUMBK[th]
                sc.wait("tensor", AC, m_sq[th][1])
                for (bk, src) in ((b1, cast_b), (b2, sq_b)):
                    m = None
                    for ic in range(IC):
                        r = sc.emit("tensor", lambda e, ic=ic, th=th, bk=bk, src=src: e.matmul(
                            pb[bk][0:1, :], ONESK[:],
                            src[:, ic * 1024 + th * 512: ic * 1024 + th * 512 + 512],
                            start=(ic == 0), stop=(ic == IC - 1)), PE if ic == IC - 1 else None)
                        if r is not None:
                            m = r
                m_sums[th] = m
            m_rb = {}
            for th in range(2):
                b1, b2 = LN_SUMBK[th]
                c1, c2 = LN_BCBK[th]
                mrow, qrow, rsrow = LNROW[th]
                sc.wait("scalar", PE, m_sums[th][1])
                sc.emit("scalar", lambda e, b1=b1, mrow=mrow: e.activation(
                    out=mrow, in_=pb[b1][0:1, :], func=AF.Copy, scale=1.0 / D), AC)
                m_q = sc.emit("scalar", lambda e, b2=b2, qrow=qrow: e.activation(
                    out=qrow, in_=pb[b2][0:1, :], func=AF.Copy, scale=1.0 / D), AC)
                sc.wait("vector", AC, m_q[1])
                m_m2 = sc.emit("vector", lambda e, mrow=mrow, rsrow=rsrow: e.tensor_tensor(
                    out=rsrow, in0=mrow, in1=mrow, op=ALU.mult), DV)
                sc.wait("vector", DV, m_m2[1])
                m_var = sc.emit("vector", lambda e, qrow=qrow, rsrow=rsrow: e.scalar_tensor_tensor(
                    out=rsrow, in0=qrow, scalar=1e-5, in1=rsrow,
                    op0=ALU.add, op1=ALU.subtract), DV)
                sc.wait("scalar", DV, m_var[1])
                m_sd = sc.emit("scalar", lambda e, rsrow=rsrow: e.activation(
                    out=rsrow, in_=rsrow, func=AF.Sqrt), AC)
                sc.wait("vector", AC, m_sd[1])
                m_rs = sc.emit("vector", lambda e, rsrow=rsrow: e.reciprocal(
                    out=rsrow, in_=rsrow), DV)
                sc.wait("tensor", DV, m_rs[1])
                sc.emit("tensor", lambda e, c1=c1, mrow=mrow: e.matmul(
                    pb[c1][:], ONESC32[:], mrow, start=True, stop=True), PE)
                m_br = sc.emit("tensor", lambda e, c2=c2, rsrow=rsrow: e.matmul(
                    pb[c2][:], ONESC32[:], rsrow, start=True, stop=True), PE)
                m_rb[th] = m_br
            mlast = None
            for th in range(2):
                c1, c2 = LN_BCBK[th]
                sc.wait("vector", PE, m_rb[th][1])
                for ic in range(IC):
                    col = ic * 1024 + th * 512
                    m_sub = sc.emit("vector", lambda e, col=col, c1=c1: e.tensor_tensor(
                        out=R2[:, col:col + 512], in0=R2[:, col:col + 512], in1=pb[c1][:],
                        op=ALU.subtract), DV)
                    sc.wait("vector", DV, m_sub[1])
                    if out_bf is not None:
                        mlast = sc.emit("vector", lambda e, col=col, c2=c2, d=out_bf: e.tensor_tensor(
                            out=d[:, col:col + 512], in0=R2[:, col:col + 512], in1=pb[c2][:],
                            op=ALU.mult), DV)
                    else:
                        mlast = sc.emit("vector", lambda e, col=col, c2=c2: e.tensor_tensor(
                            out=R2[:, col:col + 512], in0=R2[:, col:col + 512], in1=pb[c2][:],
                            op=ALU.mult), DV)
                        if store_out:
                            sc.wait("sync", DV, mlast[1])
                            sc.emit("sync", lambda e, ic=ic, th=th, col=col: e.dma_start(
                                out=out_fm_v[:, ic, th * 512:(th + 1) * 512],
                                in_=R2[:, col:col + 512]), ST, 16)
            sc.wait("scalar", DV, mlast[1])
            sc.wait("tensor", DV, mlast[1])
            return mlast

        sga_free = [None, None]

        def quant_block(src_f32, dst_bf, thrN_col, thrP_col, dst_free, bi):
            sga = SGA2[bi]
            if sga_free[bi] is not None:
                sc.wait("scalar", *sga_free[bi])
            sc.emit("scalar", lambda e, s=src_f32, c=thrN_col, sga=sga: e.activation(
                out=sga, in_=s, func=AF.Sign, bias=THR[:, c:c + 1], scale=1.0), AC)
            if dst_free is not None:
                sc.wait("scalar", *dst_free)
            m_b = sc.emit("scalar", lambda e, s=src_f32, c=thrP_col, d=dst_bf: e.activation(
                out=d, in_=s, func=AF.Sign, bias=THR[:, c:c + 1], scale=1.0), AC)
            sc.wait("vector", AC, m_b[1])
            m_w = sc.emit("vector", lambda e, d=dst_bf, sga=sga: e.tensor_tensor(
                out=d, in0=sga, in1=d, op=ALU.add), DV)
            sga_free[bi] = (DV, m_w[1])
            return (DV, m_w[1]), (AC, m_b[1])

        w1_free = list(wstream_free)
        wqt_free = [None, None]
        m_hev = {}
        m_accinit = None

        def ffn1_quant(half, hc):
            bi = hc % 2
            wsl = WS32[bi]
            if w1_free[bi] is not None:
                sc.wait("sync", *w1_free[bi])
            for ph in range(4):
                m_wd = sc.emit("sync", lambda e, half=half, hc=hc, dst=wsl, ph=ph: e.dma_start(
                    out=dst[:, ph * 512:(ph + 1) * 512],
                    in_=w1_i[:, half, hc, ph * 512:(ph + 1) * 512]), LS[bi], 16)
            sc.wait("scalar", LS[bi], m_wd[1])
            m_wq, m_ws = quant_block(wsl, WQT[bi], 0, 1, wqt_free[bi], bi)
            w1_free[bi] = m_ws
            return m_wq

        sc.wait("scalar", PE, PE.n)
        sc.wait("scalar", DV, DV.n)
        m_wq0 = ffn1_quant(0, 0)

        m_x1 = layer_norm(rbf, sqb, x1b)
        sc.wait("sync", PE, PE.n)

        for half in range(2):
            m_wq_cur = m_wq0 if half == 0 else ffn1_quant(half, 0)
            for hc in range(32):
                bi = hc % 2
                hcG = half * 32 + hc
                if half == 0 and hc == 0:
                    sc.wait("tensor", DV, m_x1[1])
                sc.wait("tensor", *m_wq_cur)
                for th in range(2):
                    if m_hev.get((bi, th)) is not None:
                        sc.wait("tensor", *m_hev[(bi, th)])
                m_h = [None, None]
                for ic in range(IC):
                    for th in range(2):
                        p_ = 2 * bi + th
                        r = sc.emit("tensor", lambda e, bi=bi, ic=ic, th=th, p_=p_: e.matmul(
                            pb[p_][:], WQT[bi][:, ic * 128:(ic + 1) * 128],
                            x1b[:, ic * 1024 + th * 512: ic * 1024 + th * 512 + 512],
                            start=(ic == 0), stop=(ic == IC - 1)),
                            PE if ic == IC - 1 else None)
                        if r is not None:
                            m_h[th] = r
                wqt_free[bi] = (PE, PE.n)
                if hc + 1 < 32:
                    m_wq_cur = ffn1_quant(half, hc + 1)
                for th in range(2):
                    p_ = 2 * bi + th
                    sc.wait("scalar", PE, m_h[th][1])
                    m_g = sc.emit("scalar", lambda e, hc=hc, th=th, hcG=hcG, p_=p_: e.activation(
                        out=h_all[:, hc * 1024 + th * 512: hc * 1024 + th * 512 + 512],
                        in_=pb[p_][:], func=AF.Gelu,
                        bias=B1S[:, hcG:hcG + 1], scale=0.5), AC)
                    m_hev[(bi, th)] = (AC, m_g[1])
            if debug_taps and half == 0:
                sc.wait("sync", AC, AC.n)
                sc.emit("sync", lambda e: e.dma_start(out=dbg_h, in_=h_all), ST, 16)
            if half == 0:
                sc.wait("vector", DV, m_x1[1])
                for oc in range(IC):
                    m_accinit = sc.emit("vector", lambda e, oc=oc: e.tensor_scalar(
                        out=R2[:, oc * 1024:(oc + 1) * 1024],
                        in0=x1b[:, oc * 1024:(oc + 1) * 1024],
                        scalar1=B2S[:, oc:oc + 1], scalar2=None, op0=ALU.add), DV)
            sc.wait("tensor", AC, AC.n)
            sc.wait("vector", DV, DV.n)
            w2_free = list(w1_free)
            m_f2evac = [None, None]

            def ffn2_quant(half, oc, hs):
                bi = hs
                wsl = WS32[bi]
                if w2_free[bi] is not None:
                    sc.wait("sync", *w2_free[bi])
                for ph in range(4):
                    m_wd = sc.emit("sync", lambda e, half=half, oc=oc, hs=hs, dst=wsl, ph=ph: e.dma_start(
                        out=dst[:, ph * 512:(ph + 1) * 512],
                        in_=w2_i[:, half, oc, hs, ph * 512:(ph + 1) * 512]), LS[bi], 16)
                sc.wait("scalar", LS[bi], m_wd[1])
                mq, ms = quant_block(wsl, WQT[bi], 2, 3, wqt_free[bi], bi)
                w2_free[bi] = ms
                wqt_free[bi] = None
                return mq

            m_wq2 = [ffn2_quant(half, 0, 0), ffn2_quant(half, 0, 1)]
            for oc in range(IC):
                for th in range(2):
                    if m_f2evac[th] is not None:
                        sc.wait("tensor", *m_f2evac[th])
                m_f = [None, None]
                m_wq2_next = [None, None]
                for hs in range(2):
                    sc.wait("tensor", *m_wq2[hs])
                    for icc in range(16):
                        hcc = hs * 16 + icc
                        for th in range(2):
                            r = sc.emit("tensor", lambda e, hs=hs, icc=icc, th=th, hcc=hcc: e.matmul(
                                pb[4 + th][:], WQT[hs][:, icc * 128:(icc + 1) * 128],
                                h_all[:, hcc * 1024 + th * 512: hcc * 1024 + th * 512 + 512],
                                start=(hcc == 0), stop=(hcc == 31)),
                                PE if icc == 15 else None)
                            if r is not None and hcc == 31:
                                m_f[th] = r
                    wqt_free[hs] = (PE, PE.n)
                    if oc + 1 < IC:
                        m_wq2_next[hs] = ffn2_quant(half, oc + 1, hs)
                for th in range(2):
                    sc.wait("vector", PE, m_f[th][1])
                    if half == 0 and oc == 0 and th == 0:
                        sc.wait("vector", DV, m_accinit[1])
                    col = oc * 1024 + th * 512
                    m_a2 = sc.emit("vector", lambda e, th=th, col=col: e.scalar_tensor_tensor(
                        out=R2[:, col:col + 512], in0=pb[4 + th][:], scalar=0.5,
                        in1=R2[:, col:col + 512], op0=ALU.mult, op1=ALU.add), DV)
                    m_f2evac[th] = (DV, m_a2[1])
                if oc + 1 < IC:
                    m_wq2 = m_wq2_next
            w1_free = list(w2_free)
            if debug_taps and half == 0:
                sc.wait("sync", DV, DV.n)
                sc.emit("sync", lambda e: e.dma_start(out=dbg_acc, in_=R2[:]), ST, 16)

        layer_norm(rbf2, sqb2, None, store_out=True)
        sc.wait("sync", ST, ST.n)

        def run_ops(e, eng):
            for op in sc.ops[eng]:
                if op[0] == "w":
                    _, s, v = op
                    e.wait_ge(s.h, v)
                else:
                    _, fn, inc, k = op
                    inst = fn(e)
                    if inc is not None:
                        inst.then_inc(inc.h, k)

        @block.sync
        def _(e):
            run_ops(e, "sync")

        @block.gpsimd
        def _(e):
            run_ops(e, "gpsimd")

        @block.scalar
        def _(e):
            run_ops(e, "scalar")

        @block.vector
        def _(e):
            run_ops(e, "vector")

        @block.tensor
        def _(e):
            run_ops(e, "tensor")

    return nc


def _rope_tables():
    theta = 1.0 / ROPE_BASE ** (np.arange(0, HD, 2, dtype=np.float32) / HD)
    t = np.arange(T, dtype=np.float32)
    ang = np.outer(theta.astype(np.float64), t.astype(np.float64))
    cos = np.cos(ang).astype(np.float32)
    sin = np.sin(ang).astype(np.float32)
    cosE = np.repeat(cos, 2, axis=0)
    sinS = np.empty((HD, T), np.float32)
    sinS[0::2] = -sin
    sinS[1::2] = sin
    return cosE, sinS


def prepare_in_maps(L0_attn_x, L0_attn_w_atten, L0_attn_w_proj, L0_ffn_w1,
                    L0_ffn_b1, L0_ffn_w2, L0_ffn_b2):
    x = np.ascontiguousarray(np.asarray(L0_attn_x, dtype=np.float32))
    wat = np.ascontiguousarray(np.asarray(L0_attn_w_atten, dtype=np.float32))
    wp = np.ascontiguousarray(np.asarray(L0_attn_w_proj, dtype=np.float32))
    w1 = np.ascontiguousarray(np.asarray(L0_ffn_w1, dtype=np.float32))
    b1 = np.asarray(L0_ffn_b1, dtype=np.float32)
    w2 = np.ascontiguousarray(np.asarray(L0_ffn_w2, dtype=np.float32))
    b2 = np.asarray(L0_ffn_b2, dtype=np.float32)

    bf = ml_dtypes.bfloat16
    x_fm = np.ascontiguousarray(
        x.transpose(2, 0, 1).reshape(IC, 128, B * T).transpose(1, 0, 2))
    wp_b = np.ascontiguousarray(
        wp.reshape(IC, 128, IC, 128).transpose(3, 0, 2, 1)
        .reshape(128, IC, 2048))
    w1_b = np.ascontiguousarray(
        w1.reshape(2, 32, 128, IC, 128).transpose(4, 0, 1, 3, 2)
        .reshape(128, 2, 32, 2048))
    w2_b = np.ascontiguousarray(
        w2.reshape(IC, 128, 2, 2, 16, 128).transpose(5, 2, 0, 3, 4, 1)
        .reshape(128, 2, IC, 2, 2048))
    b1s = np.ascontiguousarray(b1.reshape(64, 128).T)
    b2s = np.ascontiguousarray(b2.reshape(16, 128).T)
    cosE, sinS = _rope_tables()
    qc = cosE.astype(bf)
    qs = sinS.astype(bf)
    kc = cosE.astype(bf)
    ks = sinS.astype(bf)
    mask = np.zeros((128, 4, 512), np.float32)
    ktv = np.arange(128)[:, None]
    qtv = np.arange(512)[None, :]
    for p in range(4):
        mask[:, p, :] = np.where(qtv >= p * 128 + ktv, 0.0, -1e9)
    mask = np.ascontiguousarray(mask.reshape(128, 2048)).astype(bf)
    permm = np.zeros((128, 128), bf)
    for i in range(128):
        permm[i, i ^ 1] = 1
    idn = np.eye(128, dtype=np.float32).astype(bf)
    onesc = np.ones((1, 128), bf)
    onesk = np.ones((128, 1), bf)
    onesc32 = np.ones((1, 128), np.float32)
    onesk32 = np.ones((128, 1), np.float32)

    shared = {
        "x_fm": x_fm, "wp_i": wp_b, "w1_i": w1_b, "w2_i": w2_b,
        "b1s_i": b1s, "b2s_i": b2s, "qc_i": qc, "qs_i": qs, "kc_i": kc,
        "ks_i": ks, "mask_i": mask, "perm_i": permm, "idn_i": idn,
        "onesc_i": onesc,
        "onesk_i": onesk, "onesc32_i": onesc32, "onesk32_i": onesk32,
    }
    in_maps = []
    for c in range(NC):
        def _qkv(base):
            return np.ascontiguousarray(
                wat[base + (2 * c) * 128: base + (2 * c + 2) * 128, :]
                .reshape(2, 128, IC, 128).transpose(3, 2, 0, 1)
                .reshape(128, 4096))
        wq_c = _qkv(0)
        wk_c = _qkv(D)
        wv_c = _qkv(2 * D)
        x_lc = np.ascontiguousarray(x_fm[:, :, c * TOKL:(c + 1) * TOKL])
        in_maps.append(dict(shared, wq_i=wq_c, wk_i=wk_c, wv_i=wv_c,
                            x_loc=x_lc))
    return in_maps


def assemble(blks):
    out = np.empty((B, T, D), np.float32)
    for c in range(NC):
        blk = np.asarray(blks[c], dtype=np.float32)
        gt0 = c * TOKL
        out[gt0 // T, gt0 % T: gt0 % T + TOKL, :] = blk.T
    return out


def kernel(L0_attn_x, L0_attn_w_atten, L0_attn_w_proj, L0_ffn_w1, L0_ffn_b1,
           L0_ffn_w2, L0_ffn_b2):
    in_maps = prepare_in_maps(L0_attn_x, L0_attn_w_atten, L0_attn_w_proj,
                              L0_ffn_w1, L0_ffn_b1, L0_ffn_w2, L0_ffn_b2)
    nc = build_nc()
    res = run_bass_kernel_spmd(nc, in_maps, core_ids=list(range(NC)))
    return assemble([res.results[c]["out_fm"] for c in range(NC)])


# version 14
# speedup vs baseline: 1.0574x; 1.0574x over previous
import numpy as np
import ml_dtypes
from contextlib import ExitStack

import concourse.bass as bass
import concourse.mybir as mybir
from concourse.bass_utils import run_bass_kernel_spmd
from concourse.aot_env import allow_ici_all_to_all

F32 = mybir.dt.float32
BF16 = mybir.dt.bfloat16
AF = mybir.ActivationFunctionType
ALU = mybir.AluOpType
AX = mybir.AxisListType

NC = 8
B, T, D = 2, 4096, 2048
H, HD = 16, 128
IC = D // 128
FF = 8192
TOKL = 1024
ROPE_BASE = 10000.0
S128 = float(np.float32(1.0) / np.sqrt(np.float32(128.0)))
XT = 256
NXB = (B * T) // XT

ENGS = ("sync", "gpsimd", "scalar", "vector", "tensor")


class _Sem:
    __slots__ = ("name", "n", "h")

    def __init__(self, name):
        self.name = name
        self.n = 0
        self.h = None


class _Sched:
    def __init__(self):
        self.ops = {e: [] for e in ENGS}
        self.sems = []

    def sem(self, name):
        s = _Sem(name)
        self.sems.append(s)
        return s

    def wait(self, eng, s, val=None):
        v = s.n if val is None else val
        if v > 0:
            self.ops[eng].append(("w", s, v))

    def emit(self, eng, fn, inc=None, k=1):
        self.ops[eng].append(("i", fn, inc, k))
        if inc is not None:
            inc.n += k
            return (inc, inc.n)
        return None


def build_nc(debug_taps=False):
    nc = bass.Bass("TRN2", target_bir_lowering=False)

    dr = lambda n, s, d, k: nc.dram_tensor(n, s, d, kind=k).ap()
    x_fm = dr("x_fm", [128, IC, B * T], F32, "ExternalInput")
    x_loc = dr("x_loc", [128, IC, TOKL], F32, "ExternalInput")
    wq_i = dr("wq_i", [128, 4096], F32, "ExternalInput")
    wk_i = dr("wk_i", [128, 4096], F32, "ExternalInput")
    wv_i = dr("wv_i", [128, 4096], F32, "ExternalInput")
    wp_i = dr("wp_i", [128, IC, 2048], F32, "ExternalInput")
    w1_i = dr("w1_i", [128, 2, 32, 2048], F32, "ExternalInput")
    w2_i = dr("w2_i", [128, 2, IC, 2, 2048], F32, "ExternalInput")
    b1s_i = dr("b1s_i", [128, 64], F32, "ExternalInput")
    b2s_i = dr("b2s_i", [128, 16], F32, "ExternalInput")
    wsc_i = dr("wsc_i", [128, 16, 2048], F32, "ExternalInput")
    sel8_i = dr("sel8_i", [8, 2], F32, "ExternalInput")
    qc_i = dr("qc_i", [128, T], BF16, "ExternalInput")
    qs_i = dr("qs_i", [128, T], BF16, "ExternalInput")
    kc_i = dr("kc_i", [128, T], BF16, "ExternalInput")
    ks_i = dr("ks_i", [128, T], BF16, "ExternalInput")
    mask_i = dr("mask_i", [128, 4 * 512], BF16, "ExternalInput")
    perm_i = dr("perm_i", [128, 128], BF16, "ExternalInput")
    idn_i = dr("idn_i", [128, 128], BF16, "ExternalInput")
    onesc_i = dr("onesc_i", [1, 128], BF16, "ExternalInput")
    onesk_i = dr("onesk_i", [128, 1], BF16, "ExternalInput")
    onesc32_i = dr("onesc32_i", [1, 128], F32, "ExternalInput")
    onesk32_i = dr("onesk32_i", [128, 1], F32, "ExternalInput")
    out_fm = dr("out_fm", [D, TOKL], F32, "ExternalOutput")

    a2a_in = [dr(f"a2a_in{hl}", [NC * 128, TOKL], BF16, "Internal")
              for hl in range(2)]
    a2a_out = [dr(f"a2a_out{hl}", [NC * 128, TOKL], BF16, "Internal")
               for hl in range(2)]
    a2a_sc_in = dr("a2a_sc_in", [NC, 128], F32, "Internal")
    a2a_sc_out = dr("a2a_sc_out", [NC, 128], F32, "Internal")
    if debug_taps:
        dbg_h = dr("dbg_h", [128, 32768], BF16, "Internal")
        dbg_acc = dr("dbg_acc", [128, 16384], F32, "Internal")

    sc = _Sched()
    LD = sc.sem("LD")
    LS = [sc.sem("LS0"), sc.sem("LS1")]
    LX = sc.sem("LX")
    GX = [sc.sem("GX0"), sc.sem("GX1")]
    GY = [sc.sem("GY0"), sc.sem("GY1")]
    GW = sc.sem("GW")
    GP = [sc.sem("GP0"), sc.sem("GP1")]
    GF = sc.sem("GF")
    GQ = sc.sem("GQ")
    SX = sc.sem("SX")
    CCS = sc.sem("CCS")
    CC = sc.sem("CC")
    CC2 = sc.sem("CC2")
    PE = sc.sem("PE")
    AC = sc.sem("AC")
    DV = sc.sem("DV")
    ST = sc.sem("ST")

    with ExitStack() as ctx:
        sb = lambda n, s, d: ctx.enter_context(nc.sbuf_tensor(n, s, d))
        R0 = sb("R0", [128, 8192], BF16)
        R1 = sb("R1", [128, 4096], F32)
        R2 = sb("R2", [128, 16384], F32)
        R3 = sb("R3", [128, 49152], BF16)
        MK = sb("MK", [128, 2048], BF16)
        ROWS = sb("ROWS", [1, 2048], F32)
        PERM = sb("PERM", [128, 128], BF16)
        IDN = sb("IDN", [128, 128], BF16)
        ONESC = sb("ONESC", [1, 128], BF16)
        ONESK = sb("ONESK", [128, 1], BF16)
        ONESC32 = sb("ONESC32", [1, 128], F32)
        ONESK32 = sb("ONESK32", [128, 1], F32)
        B1S = sb("B1S", [128, 64], F32)
        B2S = sb("B2S", [128, 16], F32)
        THR = sb("THR", [128, 4], F32)
        SEL8 = sb("SEL8", [8, 2], F32)
        SCG = sb("SCG", [8, 128], F32)
        WSUM = sb("WSUM", [128, 128], F32)
        pb = [ctx.enter_context(nc.psum_tensor(f"pb{i}", [128, 512], F32))
              for i in range(8)]
        for s in sc.sems:
            s.h = ctx.enter_context(nc.semaphore(s.name))
        block = ctx.enter_context(nc.Block())

        R2b = R2[:].bitcast(BF16)

        QCT = R2b[:, 0:4096]
        QST = R2b[:, 4096:8192]
        KCT = R2b[:, 8192:12288]
        KST = R2b[:, 12288:16384]
        XBLK = [R2b[:, 16384:20480], R2b[:, 20480:24576]]
        WVBF = R2b[:, 24576:28672]
        RAWQ = R2b[:, 28672:28928]
        RAWK = R2b[:, 28928:29184]
        TSIN = R2[:, 14848:15104]
        TCOS = R2[:, 15104:15360]
        RLB = [R2[:, 1024:1536], R2[:, 1536:2048]]
        WQBF = R0[:, 0:4096]
        WKBF = R0[:, 4096:8192]
        PBUF6 = [R0[:, i * 512:(i + 1) * 512] for i in range(6)]
        YOUT = [R0[:, 3072 + i * 512: 3584 + i * 512] for i in range(2)]
        WPBF = [R0[:, i * 2048:(i + 1) * 2048] for i in range(2)]
        SGA2 = [R0[:, 0:2048], R0[:, 2048:4096]]
        WQT = [R0[:, 4096:6144], R0[:, 6144:8192]]
        WS32 = [R1[:, 0:2048], R1[:, 2048:4096]]
        MB_T = [R1[:, 0:512], R1[:, 1024:1536]]
        RB_T = [R1[:, 512:1024], R1[:, 1536:2048]]
        qb = R3[:, 0:16384]
        kb = R3[:, 16384:32768]
        vb = R3[:, 32768:49152]
        y_fm = R3[:, 0:16384]
        rbf = R3[:, 16384:32768]
        x1b = R3[:, 32768:49152]
        sqb = R3[:, 0:16384]
        h_all = R3[:, 0:32768]
        rbf2 = R3[:, 0:16384]
        sqb2 = R3[:, 16384:32768]
        RLROW = [ROWS[:, 0:512], ROWS[:, 512:1024]]
        LNROW = [[ROWS[:, 512 + k * 512: 1024 + k * 512] for k in range(3)],
                 [R1[0:1, 2048 + k * 512: 2560 + k * 512] for k in range(3)]]
        TINY = ROWS[:, 0:1]

        m_consts = None
        for (dst, src) in [
            (QCT, qc_i), (QST, qs_i), (KCT, kc_i), (KST, ks_i),
            (MK[:], mask_i), (PERM[:], perm_i), (IDN[:], idn_i),
            (ONESC[:], onesc_i),
            (ONESK[:], onesk_i), (ONESC32[:], onesc32_i),
            (ONESK32[:], onesk32_i), (B1S[:], b1s_i), (B2S[:], b2s_i),
            (SEL8[:], sel8_i),
        ]:
            m_consts = sc.emit("sync", lambda e, d=dst, s=src: e.dma_start(out=d, in_=s), LD, 16)
        sc.emit("gpsimd", lambda e: e.dma_start(out=WQBF, in_=wq_i), GW, 16)
        sc.emit("gpsimd", lambda e: e.dma_start(out=WKBF, in_=wk_i), GW, 16)
        m_wqkv = sc.emit("gpsimd", lambda e: e.dma_start(out=WVBF, in_=wv_i), GW, 16)

        N_SC_BLK = 16

        sblk = {"i": 0}
        wstream_free = [None, None]

        def emit_scale_blocks(kk):
            for _ in range(kk):
                i = sblk["i"]
                if i >= N_SC_BLK:
                    return
                bi = i % 2
                wsl = WS32[bi]
                if wstream_free[bi] is not None:
                    sc.wait("sync", *wstream_free[bi])
                src = wsc_i[:, i, :]
                for ph in range(2):
                    m_d = sc.emit("sync", lambda e, dst=wsl, s=src, ph=ph: e.dma_start(
                        out=dst[:, ph * 1024:(ph + 1) * 1024],
                        in_=s[:, ph * 1024:(ph + 1) * 1024]), LS[bi], 16)
                sc.wait("vector", LS[bi], m_d[1])
                col = 4 * i
                for ch in range(4):
                    sc.emit("vector", lambda e, wsl=wsl, c=col + ch, ch=ch: e.tensor_reduce(
                        out=WSUM[:, c:c + 1], in_=wsl[:, ch * 512:(ch + 1) * 512],
                        axis=AX.X, op=ALU.add, apply_absolute_value=True), DV)
                wstream_free[bi] = (DV, DV.n)
                sblk["i"] += 1

        m_xblk = [None] * NXB
        xblk_free = [None, None]
        sc.wait("tensor", GW, m_wqkv[1])
        sc.wait("tensor", LD, m_consts[1])

        m_rawfree = {"q": None, "k": None}
        m_qkps_free = {"q": None, "k": None}
        m_permfree = [None, None]
        m_vps_free = [None] * 4
        last_rope_dve = None

        for tb in range(NXB):
            bi = tb % 2
            b = tb // 16
            pos0 = (tb % 16) * XT
            if xblk_free[bi] is not None:
                sc.wait("gpsimd", *xblk_free[bi])
            m_xblk[tb] = sc.emit("gpsimd", lambda e, bi=bi, tb=tb: e.dma_start(
                out=XBLK[bi].rearrange("p (ic t) -> p ic t", ic=IC),
                in_=x_fm[:, :, tb * XT:(tb + 1) * XT]),
                GX[bi], 16)
            sc.wait("tensor", GX[bi], m_xblk[tb][1])
            for hl in range(2):
                bh = hl * 2 + b
                for which, wbf, psl, raw, ctab, stab, dstq in (
                    ("q", WQBF, pb[0], RAWQ, QCT, QST, qb),
                    ("k", WKBF, pb[1], RAWK, KCT, KST, kb),
                ):
                    if m_qkps_free[which] is not None:
                        sc.wait("tensor", *m_qkps_free[which])
                    m_mm = None
                    for ic in range(IC):
                        r = sc.emit("tensor", lambda e, w=wbf, ic=ic, hl=hl, bi=bi, ps=psl: e.matmul(
                            ps[:, 0:XT],
                            w[:, (ic * 2 + hl) * 128:(ic * 2 + hl + 1) * 128],
                            XBLK[bi][:, ic * XT:(ic + 1) * XT],
                            start=(ic == 0), stop=(ic == IC - 1)),
                            PE if ic == IC - 1 else None)
                        if r is not None:
                            m_mm = r
                    if m_rawfree[which] is not None:
                        sc.wait("scalar", *m_rawfree[which])
                    sc.wait("scalar", PE, m_mm[1])
                    m_raw = sc.emit("scalar", lambda e, ps=psl, raw=raw: e.activation(
                        out=raw, in_=ps[:, 0:XT], func=AF.Copy), AC)
                    m_qkps_free[which] = (AC, m_raw[1])
                    if m_permfree[hl] is not None:
                        sc.wait("tensor", *m_permfree[hl])
                    sc.wait("tensor", AC, m_raw[1])
                    m_sw = sc.emit("tensor", lambda e, raw=raw, hl=hl: e.matmul(
                        pb[2 + hl][:, 0:XT], PERM[:], raw, start=True, stop=True), PE)
                    sc.wait("vector", PE, m_sw[1])
                    if last_rope_dve is not None:
                        sc.wait("vector", DV, last_rope_dve[1])
                    sc.emit("vector", lambda e, hl=hl, st=stab, pos0=pos0: e.tensor_tensor(
                        out=TSIN, in0=pb[2 + hl][:, 0:XT], in1=st[:, pos0:pos0 + XT], op=ALU.mult), DV)
                    m_tc = sc.emit("vector", lambda e, raw=raw, ct=ctab, pos0=pos0: e.tensor_tensor(
                        out=TCOS, in0=raw, in1=ct[:, pos0:pos0 + XT], op=ALU.mult), DV)
                    sc.wait("vector", DV, m_tc[1])
                    m_rot = sc.emit("vector", lambda e, dq=dstq, bh=bh, pos0=pos0: e.tensor_tensor(
                        out=dq[:, bh * 4096 + pos0: bh * 4096 + pos0 + XT],
                        in0=TSIN, in1=TCOS, op=ALU.add), DV)
                    last_rope_dve = m_rot
                    m_rawfree[which] = (DV, m_rot[1])
                    m_permfree[hl] = (DV, m_rot[1])
                if hl == 0:
                    for s_ in range(2):
                        vslot = (tb * 2 + s_) % 4
                        psv = pb[4 + vslot]
                        if m_vps_free[vslot] is not None:
                            sc.wait("tensor", *m_vps_free[vslot])
                        m_mv = None
                        for ic in range(IC):
                            r = sc.emit("tensor", lambda e, s_=s_, ic=ic, bi=bi, psv=psv: e.matmul(
                                psv[:, 0:256],
                                XBLK[bi][:, ic * XT + s_ * 128: ic * XT + s_ * 128 + 128],
                                WVBF[:, ic * 256:(ic + 1) * 256],
                                start=(ic == 0), stop=(ic == IC - 1)),
                                PE if ic == IC - 1 else None)
                            if r is not None:
                                m_mv = r
                        sc.wait("scalar", PE, m_mv[1])
                        jsub = (tb % 16) * 2 + s_
                        m_ve = None
                        for hl2 in range(2):
                            bh2 = hl2 * 2 + b
                            m_ve = sc.emit("scalar", lambda e, psv=psv, bh2=bh2, jsub=jsub, hl2=hl2: e.activation(
                                out=vb[:, bh2 * 4096 + jsub * 128: bh2 * 4096 + jsub * 128 + 128],
                                in_=psv[:, hl2 * 128:(hl2 + 1) * 128], func=AF.Copy), AC)
                        m_vps_free[vslot] = (AC, m_ve[1])
            xblk_free[bi] = (PE, PE.n)
            emit_scale_blocks(2)

        m_ps = sc.emit("vector", lambda e: e.tensor_reduce(
            out=WSUM[:, 64:65], in_=WSUM[:, 0:64], axis=AX.X, op=ALU.add), DV)
        sc.wait("tensor", DV, m_ps[1])
        m_pmm = sc.emit("tensor", lambda e: e.matmul(
            pb[7][0:1, 0:1], ONESK32[:], WSUM[:, 64:65], start=True, stop=True), PE)
        sc.wait("scalar", PE, m_pmm[1])
        m_pc = sc.emit("scalar", lambda e: e.activation(
            out=WSUM[0:1, 65:66], in_=pb[7][0:1, 0:1], func=AF.Copy), AC)
        sc.wait("sync", AC, m_pc[1])
        m_sd = None
        for dcore in range(NC):
            m_sd = sc.emit("sync", lambda e, d=dcore: e.dma_start(
                out=a2a_sc_in[d:d + 1, 0:1], in_=WSUM[0:1, 65:66]), SX, 16)
        sc.wait("gpsimd", SX, m_sd[1])

        def _ccs(e):
            with allow_ici_all_to_all():
                return e.collective_compute(
                    "AllToAll", ALU.bypass,
                    replica_groups=[list(range(NC))],
                    ins=[a2a_sc_in.opt()], outs=[a2a_sc_out.opt()])

        sc.emit("gpsimd", _ccs, CCS)

        sc.wait("tensor", DV, DV.n)
        sc.wait("tensor", AC, AC.n)

        LOOKA = 3
        exp_done = [None] * 4
        slot_free = [None] * 6
        yp_free = [None, None]
        rl_free = [None, None]
        rlb_free = [None, None]
        yout_free = [None, None]
        rl_inuse = [None, None]
        pb7_free = [(AC, m_pc[1])]
        pend = []
        fin_pend = []

        y_fm_hv = y_fm.rearrange("p (s two t) -> p s two t", s=8, two=2)

        def emit_half_exchange(hl):
            sc.wait("gpsimd", GY[0], GY[0].n)
            sc.wait("gpsimd", GY[1], GY[1].n)

            def _cc(e, hl=hl):
                with allow_ici_all_to_all():
                    return e.collective_compute(
                        "AllToAll", ALU.bypass,
                        replica_groups=[list(range(NC))],
                        ins=[a2a_in[hl].opt()], outs=[a2a_out[hl].opt()])

            sc.emit("gpsimd", _cc, CC if hl == 0 else CC2)

        def fin_stage(f):
            par = f["par"]
            st = f["stage"]
            if st == 0:
                rl = RLROW[par]
                sc.wait("vector", PE, f["lstop"])
                if rl_inuse[par] is not None:
                    sc.wait("vector", *rl_inuse[par])
                f["m_r"] = sc.emit("vector", lambda e, par=par, rl=rl: e.reciprocal(
                    out=rl, in_=pb[6][64 * par:64 * par + 1, :]), DV)
                rl_free[par] = (DV, f["m_r"][1])
            elif st == 1:
                rl = RLROW[par]
                if pb7_free[0] is not None:
                    sc.wait("tensor", *pb7_free[0])
                sc.wait("tensor", DV, f["m_r"][1])
                f["m_bc"] = sc.emit("tensor", lambda e, rl=rl: e.matmul(
                    pb[7][:], ONESC32[:], rl, start=True, stop=True), PE)
                rl_inuse[par] = (PE, f["m_bc"][1])
            elif st == 2:
                if rlb_free[par] is not None:
                    sc.wait("scalar", *rlb_free[par])
                sc.wait("scalar", PE, f["m_bc"][1])
                f["m_cp"] = sc.emit("scalar", lambda e, par=par: e.activation(
                    out=RLB[par], in_=pb[7][:], func=AF.Copy), AC)
                pb7_free[0] = (AC, f["m_cp"][1])
            else:
                if yout_free[par] is not None:
                    sc.wait("vector", *yout_free[par])
                sc.wait("vector", AC, f["m_cp"][1])
                m_yo = sc.emit("vector", lambda e, par=par, ypb=f["ypb"]: e.tensor_tensor(
                    out=YOUT[par], in0=ypb[:], in1=RLB[par], op=ALU.mult), DV)
                yp_free[par] = (DV, m_yo[1])
                rlb_free[par] = (DV, m_yo[1])
                sc.wait("sync", DV, m_yo[1])
                r0 = (f["b"] * 4 + f["qtb"] // 2) * 128
                c0 = (f["qtb"] % 2) * 512
                m_st = sc.emit("sync", lambda e, r0=r0, c0=c0, par=par, hl=f["hl"]: e.dma_start(
                    out=a2a_in[hl][r0:r0 + 128, c0:c0 + 512], in_=YOUT[par]), GY[par], 16)
                yout_free[par] = (GY[par], m_st[1])
                if f["hl"] == 0 and f["b"] == 1 and f["qtb"] == 7:
                    emit_half_exchange(0)
                emit_scale_blocks(4)
            f["stage"] += 1

        def pop_one():
            r = pend.pop(0)
            while fin_pend:
                f0 = fin_pend[0]
                if f0["stage"] < 4 and f0["age"] >= f0["stage"] + 1:
                    fin_stage(f0)
                    if f0["stage"] == 4:
                        fin_pend.pop(0)
                        continue
                break
            for f in fin_pend:
                f["age"] += 1
            par = r["par"]
            sc.wait("tensor", AC, r["expv"])
            if r["j"] == 0:
                if yp_free[par] is not None:
                    sc.wait("tensor", *yp_free[par])
                if rl_free[par] is not None:
                    sc.wait("tensor", *rl_free[par])
            sc.emit("tensor", lambda e, r=r: e.matmul(
                r["ypb"][:],
                vb[:, r["bh"] * 4096 + r["j"] * 128: r["bh"] * 4096 + (r["j"] + 1) * 128],
                PBUF6[r["sl"]], start=(r["j"] == 0), stop=(r["j"] == r["n"] - 1)), None)
            m_l = sc.emit("tensor", lambda e, r=r: e.matmul(
                pb[6][64 * r["par"]:64 * r["par"] + 1, :], ONESK[:], PBUF6[r["sl"]],
                start=(r["j"] == 0), stop=(r["j"] == r["n"] - 1)), PE)
            slot_free[r["sl"]] = (PE, m_l[1])
            if r["j"] == r["n"] - 1:
                fin_pend.append(dict(age=0, stage=0, par=par, ypb=r["ypb"],
                                     lstop=m_l[1], hl=r["hl"], b=r["b"],
                                     qtb=r["qtb"]))

        gj = 0
        for (hl, b) in [(0, 0), (0, 1), (1, 0), (1, 1)]:
            bh = hl * 2 + b
            for qtb in range(8):
                n = 4 * qtb + 4
                par = qtb % 2
                ypb = pb[4 + par]
                for j in range(n):
                    sb_i = gj % 4
                    sl_i = gj % 6
                    sp = pb[sb_i]
                    diag = j >= n - 4
                    if exp_done[sb_i] is not None:
                        sc.wait("tensor", *exp_done[sb_i])
                    m_s = sc.emit("tensor", lambda e, sp=sp, bh=bh, j=j, qtb=qtb, diag=diag: e.matmul(
                        sp[:], kb[:, bh * 4096 + j * 128: bh * 4096 + (j + 1) * 128],
                        qb[:, bh * 4096 + qtb * 512: bh * 4096 + (qtb + 1) * 512],
                        start=True, stop=(not diag)), None if diag else PE)
                    if diag:
                        p_ = j - (n - 4)
                        m_s = sc.emit("tensor", lambda e, sp=sp, p_=p_: e.matmul(
                            sp[:], IDN[:], MK[:, p_ * 512:(p_ + 1) * 512],
                            start=False, stop=True), PE)
                    if slot_free[sl_i] is not None:
                        sc.wait("scalar", *slot_free[sl_i])
                    sc.wait("scalar", PE, m_s[1])
                    m_e = sc.emit("scalar", lambda e, sp=sp, sl_i=sl_i: e.activation(
                        out=PBUF6[sl_i], in_=sp[:], func=AF.Exp, scale=S128), AC)
                    exp_done[sb_i] = (AC, m_e[1])
                    pend.append(dict(sl=sl_i, expv=m_e[1], par=par, bh=bh,
                                     j=j, n=n, ypb=ypb, hl=hl, b=b, qtb=qtb))
                    gj += 1
                    if len(pend) > LOOKA:
                        pop_one()
        while pend:
            pop_one()
        while fin_pend:
            f0 = fin_pend.pop(0)
            while f0["stage"] < 4:
                fin_stage(f0)
        emit_scale_blocks(N_SC_BLK)

        emit_half_exchange(1)
        sc.wait("gpsimd", CC, 1)
        sc.wait("gpsimd", CC2, 1)
        m_yfm = None
        for hl2 in range(2):
            m_yfm = sc.emit("gpsimd", lambda e, hl2=hl2: e.dma_start(
                out=y_fm_hv[:, :, hl2, :],
                in_=a2a_out[hl2].rearrange("(s p) t -> p s t", p=128)), GF, 16)

        sc.wait("gpsimd", CCS, 1)
        m_scg = sc.emit("gpsimd", lambda e: e.dma_start(
            out=SCG[:], in_=a2a_sc_out), GQ, 16)
        sc.wait("tensor", GQ, m_scg[1])
        sc.wait("tensor", AC, AC.n)
        m_sel = None
        for gcol in range(2):
            m_sel = sc.emit("tensor", lambda e, g=gcol: e.matmul(
                pb[7][0:1, g:g + 1], SEL8[:, g:g + 1], SCG[:, 0:1],
                start=True, stop=True), PE)
        m_cp2 = None
        for (gcol, cth) in [(0, 0), (1, 2)]:
            if m_cp2 is not None:
                sc.wait("tensor", AC, m_cp2[1])
            sc.wait("scalar", PE, m_sel[1])
            sc.wait("scalar", AC, AC.n)
            m_cp = sc.emit("scalar", lambda e, g=gcol: e.activation(
                out=TINY, in_=pb[7][0:1, g:g + 1], func=AF.Copy,
                scale=0.5 / (FF * D)), AC)
            sc.wait("tensor", AC, m_cp[1])
            m_mm2 = sc.emit("tensor", lambda e: e.matmul(
                pb[7][:, 2:3], ONESC32[:], TINY, start=True, stop=True), PE)
            sc.wait("scalar", PE, m_mm2[1])
            m_cp2 = sc.emit("scalar", lambda e, ct=cth: e.activation(
                out=THR[:, ct + 1:ct + 2], in_=pb[7][:, 2:3], func=AF.Copy), AC)
            sc.wait("vector", AC, m_cp2[1])
            sc.emit("vector", lambda e, ct=cth: e.tensor_scalar(
                out=THR[:, ct:ct + 1], in0=THR[:, ct + 1:ct + 2],
                scalar1=-1.0, scalar2=None, op0=ALU.mult), DV)

        sc.wait("sync", DV, DV.n)
        m_xloc = sc.emit("sync", lambda e: e.dma_start(
            out=R2[:].rearrange("p (a t) -> p a t", a=IC),
            in_=x_loc), LX, 16)
        m_wp = [None] * IC
        sc.wait("gpsimd", DV, DV.n)

        def emit_wp(oc):
            bi = oc % 2
            m_wp[oc] = sc.emit("gpsimd", lambda e, oc=oc, bi=bi: e.dma_start(
                out=WPBF[bi], in_=wp_i[:, oc, :]), GP[bi], 16)

        emit_wp(0)
        emit_wp(1)
        sc.wait("tensor", GF, m_yfm[1])
        sc.wait("vector", LX, m_xloc[1])
        m_prev_evac = [None, None]
        for oc in range(IC):
            bi = oc % 2
            sc.wait("tensor", GP[bi], m_wp[oc][1])
            for th in range(2):
                if m_prev_evac[th] is not None:
                    sc.wait("tensor", *m_prev_evac[th])
            m_p = [None, None]
            for ic in range(IC):
                for th in range(2):
                    r = sc.emit("tensor", lambda e, bi=bi, ic=ic, th=th: e.matmul(
                        pb[th][:], WPBF[bi][:, ic * 128:(ic + 1) * 128],
                        y_fm[:, ic * 1024 + th * 512: ic * 1024 + th * 512 + 512],
                        start=(ic == 0), stop=(ic == IC - 1)),
                        PE if ic == IC - 1 else None)
                    if r is not None:
                        m_p[th] = r
            for th in range(2):
                sc.wait("vector", PE, m_p[th][1])
                col = oc * 1024 + th * 512
                m_r1 = sc.emit("vector", lambda e, th=th, col=col: e.tensor_tensor(
                    out=R2[:, col:col + 512], in0=pb[th][:], in1=R2[:, col:col + 512],
                    op=ALU.add), DV)
                m_prev_evac[th] = (DV, m_r1[1])
            if oc + 2 < IC:
                sc.wait("gpsimd", PE, PE.n)
                emit_wp(oc + 2)

        LN_SUMBK = ((2, 3), (6, 7))
        LN_BCBK = ((4, 5), (0, 1))

        out_fm_v = out_fm.rearrange("(a p) t -> p a t", p=128)

        def layer_norm(cast_b, sq_b, out_bf, store_out=False):
            sc.wait("scalar", DV, DV.n)
            sc.wait("scalar", AC, AC.n)
            sc.wait("scalar", PE, PE.n)
            sc.wait("tensor", AC, AC.n)
            sc.wait("tensor", DV, DV.n)
            R2v = R2[:].rearrange("p (a t) -> p a t", a=IC)
            cbv = cast_b.rearrange("p (a t) -> p a t", a=IC)
            sqv = sq_b.rearrange("p (a t) -> p a t", a=IC)
            m_sq = {}
            for th in range(2):
                sl = slice(th * 512, th * 512 + 512)
                sc.emit("scalar", lambda e, sl=sl: e.activation(
                    out=cbv[:, :, sl], in_=R2v[:, :, sl], func=AF.Copy), AC)
                m_sq[th] = sc.emit("scalar", lambda e, sl=sl: e.activation(
                    out=sqv[:, :, sl], in_=R2v[:, :, sl], func=AF.Square), AC)
            m_sums = {}
            for th in range(2):
                b1, b2 = LN_SUMBK[th]
                sc.wait("tensor", AC, m_sq[th][1])
                for (bk, src) in ((b1, cast_b), (b2, sq_b)):
                    m = None
                    for ic in range(IC):
                        r = sc.emit("tensor", lambda e, ic=ic, th=th, bk=bk, src=src: e.matmul(
                            pb[bk][0:1, :], ONESK[:],
                            src[:, ic * 1024 + th * 512: ic * 1024 + th * 512 + 512],
                            start=(ic == 0), stop=(ic == IC - 1)), PE if ic == IC - 1 else None)
                        if r is not None:
                            m = r
                m_sums[th] = m
            m_rb = {}
            for th in range(2):
                b1, b2 = LN_SUMBK[th]
                c1, c2 = LN_BCBK[th]
                mrow, qrow, rsrow = LNROW[th]
                sc.wait("scalar", PE, m_sums[th][1])
                sc.emit("scalar", lambda e, b1=b1, mrow=mrow: e.activation(
                    out=mrow, in_=pb[b1][0:1, :], func=AF.Copy, scale=1.0 / D), AC)
                m_q = sc.emit("scalar", lambda e, b2=b2, qrow=qrow: e.activation(
                    out=qrow, in_=pb[b2][0:1, :], func=AF.Copy, scale=1.0 / D), AC)
                sc.wait("vector", AC, m_q[1])
                m_m2 = sc.emit("vector", lambda e, mrow=mrow, rsrow=rsrow: e.tensor_tensor(
                    out=rsrow, in0=mrow, in1=mrow, op=ALU.mult), DV)
                sc.wait("vector", DV, m_m2[1])
                m_var = sc.emit("vector", lambda e, qrow=qrow, rsrow=rsrow: e.scalar_tensor_tensor(
                    out=rsrow, in0=qrow, scalar=1e-5, in1=rsrow,
                    op0=ALU.add, op1=ALU.subtract), DV)
                sc.wait("scalar", DV, m_var[1])
                m_sd = sc.emit("scalar", lambda e, rsrow=rsrow: e.activation(
                    out=rsrow, in_=rsrow, func=AF.Sqrt), AC)
                sc.wait("vector", AC, m_sd[1])
                m_rs = sc.emit("vector", lambda e, rsrow=rsrow: e.reciprocal(
                    out=rsrow, in_=rsrow), DV)
                sc.wait("tensor", DV, m_rs[1])
                sc.emit("tensor", lambda e, c1=c1, mrow=mrow: e.matmul(
                    pb[c1][:], ONESC32[:], mrow, start=True, stop=True), PE)
                m_br = sc.emit("tensor", lambda e, c2=c2, rsrow=rsrow: e.matmul(
                    pb[c2][:], ONESC32[:], rsrow, start=True, stop=True), PE)
                m_rb[th] = m_br
            mlast = None
            for th in range(2):
                c1, c2 = LN_BCBK[th]
                sc.wait("vector", PE, m_rb[th][1])
                for ic in range(IC):
                    col = ic * 1024 + th * 512
                    m_sub = sc.emit("vector", lambda e, col=col, c1=c1: e.tensor_tensor(
                        out=R2[:, col:col + 512], in0=R2[:, col:col + 512], in1=pb[c1][:],
                        op=ALU.subtract), DV)
                    sc.wait("vector", DV, m_sub[1])
                    if out_bf is not None:
                        mlast = sc.emit("vector", lambda e, col=col, c2=c2, d=out_bf: e.tensor_tensor(
                            out=d[:, col:col + 512], in0=R2[:, col:col + 512], in1=pb[c2][:],
                            op=ALU.mult), DV)
                    else:
                        mlast = sc.emit("vector", lambda e, col=col, c2=c2: e.tensor_tensor(
                            out=R2[:, col:col + 512], in0=R2[:, col:col + 512], in1=pb[c2][:],
                            op=ALU.mult), DV)
                        if store_out:
                            sc.wait("sync", DV, mlast[1])
                            sc.emit("sync", lambda e, ic=ic, th=th, col=col: e.dma_start(
                                out=out_fm_v[:, ic, th * 512:(th + 1) * 512],
                                in_=R2[:, col:col + 512]), ST, 16)
            sc.wait("scalar", DV, mlast[1])
            sc.wait("tensor", DV, mlast[1])
            return mlast

        sga_free = [None, None]

        def quant_block(src_f32, dst_bf, thrN_col, thrP_col, dst_free, bi):
            sga = SGA2[bi]
            if sga_free[bi] is not None:
                sc.wait("scalar", *sga_free[bi])
            sc.emit("scalar", lambda e, s=src_f32, c=thrN_col, sga=sga: e.activation(
                out=sga, in_=s, func=AF.Sign, bias=THR[:, c:c + 1], scale=1.0), AC)
            if dst_free is not None:
                sc.wait("scalar", *dst_free)
            m_b = sc.emit("scalar", lambda e, s=src_f32, c=thrP_col, d=dst_bf: e.activation(
                out=d, in_=s, func=AF.Sign, bias=THR[:, c:c + 1], scale=1.0), AC)
            sc.wait("vector", AC, m_b[1])
            m_w = sc.emit("vector", lambda e, d=dst_bf, sga=sga: e.tensor_tensor(
                out=d, in0=sga, in1=d, op=ALU.add), DV)
            sga_free[bi] = (DV, m_w[1])
            return (DV, m_w[1]), (AC, m_b[1])

        w1_free = list(wstream_free)
        wqt_free = [None, None]
        m_hev = {}
        m_accinit = None

        def ffn1_quant(half, hc):
            bi = hc % 2
            wsl = WS32[bi]
            if w1_free[bi] is not None:
                sc.wait("sync", *w1_free[bi])
            for ph in range(4):
                m_wd = sc.emit("sync", lambda e, half=half, hc=hc, dst=wsl, ph=ph: e.dma_start(
                    out=dst[:, ph * 512:(ph + 1) * 512],
                    in_=w1_i[:, half, hc, ph * 512:(ph + 1) * 512]), LS[bi], 16)
            sc.wait("scalar", LS[bi], m_wd[1])
            m_wq, m_ws = quant_block(wsl, WQT[bi], 0, 1, wqt_free[bi], bi)
            w1_free[bi] = m_ws
            return m_wq

        sc.wait("scalar", PE, PE.n)
        sc.wait("scalar", DV, DV.n)
        m_wq0 = ffn1_quant(0, 0)

        m_x1 = layer_norm(rbf, sqb, x1b)
        sc.wait("sync", PE, PE.n)

        for half in range(2):
            m_wq_cur = m_wq0 if half == 0 else ffn1_quant(half, 0)
            for hc in range(32):
                bi = hc % 2
                hcG = half * 32 + hc
                if half == 0 and hc == 0:
                    sc.wait("tensor", DV, m_x1[1])
                sc.wait("tensor", *m_wq_cur)
                for th in range(2):
                    if m_hev.get((bi, th)) is not None:
                        sc.wait("tensor", *m_hev[(bi, th)])
                m_h = [None, None]
                for ic in range(IC):
                    for th in range(2):
                        p_ = 2 * bi + th
                        r = sc.emit("tensor", lambda e, bi=bi, ic=ic, th=th, p_=p_: e.matmul(
                            pb[p_][:], WQT[bi][:, ic * 128:(ic + 1) * 128],
                            x1b[:, ic * 1024 + th * 512: ic * 1024 + th * 512 + 512],
                            start=(ic == 0), stop=(ic == IC - 1)),
                            PE if ic == IC - 1 else None)
                        if r is not None:
                            m_h[th] = r
                wqt_free[bi] = (PE, PE.n)
                if hc + 1 < 32:
                    m_wq_cur = ffn1_quant(half, hc + 1)
                for th in range(2):
                    p_ = 2 * bi + th
                    sc.wait("scalar", PE, m_h[th][1])
                    m_g = sc.emit("scalar", lambda e, hc=hc, th=th, hcG=hcG, p_=p_: e.activation(
                        out=h_all[:, hc * 1024 + th * 512: hc * 1024 + th * 512 + 512],
                        in_=pb[p_][:], func=AF.Gelu,
                        bias=B1S[:, hcG:hcG + 1], scale=0.5), AC)
                    m_hev[(bi, th)] = (AC, m_g[1])
            if debug_taps and half == 0:
                sc.wait("sync", AC, AC.n)
                sc.emit("sync", lambda e: e.dma_start(out=dbg_h, in_=h_all), ST, 16)
            if half == 0:
                sc.wait("vector", DV, m_x1[1])
                for oc in range(IC):
                    m_accinit = sc.emit("vector", lambda e, oc=oc: e.tensor_scalar(
                        out=R2[:, oc * 1024:(oc + 1) * 1024],
                        in0=x1b[:, oc * 1024:(oc + 1) * 1024],
                        scalar1=B2S[:, oc:oc + 1], scalar2=None, op0=ALU.add), DV)
            sc.wait("tensor", AC, AC.n)
            sc.wait("vector", DV, DV.n)
            w2_free = list(w1_free)
            m_f2evac = [None, None]

            def ffn2_quant(half, oc, hs):
                bi = hs
                wsl = WS32[bi]
                if w2_free[bi] is not None:
                    sc.wait("sync", *w2_free[bi])
                for ph in range(4):
                    m_wd = sc.emit("sync", lambda e, half=half, oc=oc, hs=hs, dst=wsl, ph=ph: e.dma_start(
                        out=dst[:, ph * 512:(ph + 1) * 512],
                        in_=w2_i[:, half, oc, hs, ph * 512:(ph + 1) * 512]), LS[bi], 16)
                sc.wait("scalar", LS[bi], m_wd[1])
                mq, ms = quant_block(wsl, WQT[bi], 2, 3, wqt_free[bi], bi)
                w2_free[bi] = ms
                wqt_free[bi] = None
                return mq

            m_wq2 = [ffn2_quant(half, 0, 0), ffn2_quant(half, 0, 1)]
            for oc in range(IC):
                for th in range(2):
                    if m_f2evac[th] is not None:
                        sc.wait("tensor", *m_f2evac[th])
                m_f = [None, None]
                m_wq2_next = [None, None]
                for hs in range(2):
                    sc.wait("tensor", *m_wq2[hs])
                    for icc in range(16):
                        hcc = hs * 16 + icc
                        for th in range(2):
                            r = sc.emit("tensor", lambda e, hs=hs, icc=icc, th=th, hcc=hcc: e.matmul(
                                pb[4 + th][:], WQT[hs][:, icc * 128:(icc + 1) * 128],
                                h_all[:, hcc * 1024 + th * 512: hcc * 1024 + th * 512 + 512],
                                start=(hcc == 0), stop=(hcc == 31)),
                                PE if icc == 15 else None)
                            if r is not None and hcc == 31:
                                m_f[th] = r
                    wqt_free[hs] = (PE, PE.n)
                    if oc + 1 < IC:
                        m_wq2_next[hs] = ffn2_quant(half, oc + 1, hs)
                for th in range(2):
                    sc.wait("vector", PE, m_f[th][1])
                    if half == 0 and oc == 0 and th == 0:
                        sc.wait("vector", DV, m_accinit[1])
                    col = oc * 1024 + th * 512
                    m_a2 = sc.emit("vector", lambda e, th=th, col=col: e.scalar_tensor_tensor(
                        out=R2[:, col:col + 512], in0=pb[4 + th][:], scalar=0.5,
                        in1=R2[:, col:col + 512], op0=ALU.mult, op1=ALU.add), DV)
                    m_f2evac[th] = (DV, m_a2[1])
                if oc + 1 < IC:
                    m_wq2 = m_wq2_next
            w1_free = list(w2_free)
            if debug_taps and half == 0:
                sc.wait("sync", DV, DV.n)
                sc.emit("sync", lambda e: e.dma_start(out=dbg_acc, in_=R2[:]), ST, 16)

        layer_norm(rbf2, sqb2, None, store_out=True)
        sc.wait("sync", ST, ST.n)

        def run_ops(e, eng):
            for op in sc.ops[eng]:
                if op[0] == "w":
                    _, s, v = op
                    e.wait_ge(s.h, v)
                else:
                    _, fn, inc, k = op
                    inst = fn(e)
                    if inc is not None:
                        inst.then_inc(inc.h, k)

        @block.sync
        def _(e):
            run_ops(e, "sync")

        @block.gpsimd
        def _(e):
            run_ops(e, "gpsimd")

        @block.scalar
        def _(e):
            run_ops(e, "scalar")

        @block.vector
        def _(e):
            run_ops(e, "vector")

        @block.tensor
        def _(e):
            run_ops(e, "tensor")

    return nc


def _rope_tables():
    theta = 1.0 / ROPE_BASE ** (np.arange(0, HD, 2, dtype=np.float32) / HD)
    t = np.arange(T, dtype=np.float32)
    ang = np.outer(theta.astype(np.float64), t.astype(np.float64))
    cos = np.cos(ang).astype(np.float32)
    sin = np.sin(ang).astype(np.float32)
    cosE = np.repeat(cos, 2, axis=0)
    sinS = np.empty((HD, T), np.float32)
    sinS[0::2] = -sin
    sinS[1::2] = sin
    return cosE, sinS


def prepare_in_maps(L0_attn_x, L0_attn_w_atten, L0_attn_w_proj, L0_ffn_w1,
                    L0_ffn_b1, L0_ffn_w2, L0_ffn_b2):
    x = np.ascontiguousarray(np.asarray(L0_attn_x, dtype=np.float32))
    wat = np.ascontiguousarray(np.asarray(L0_attn_w_atten, dtype=np.float32))
    wp = np.ascontiguousarray(np.asarray(L0_attn_w_proj, dtype=np.float32))
    w1 = np.ascontiguousarray(np.asarray(L0_ffn_w1, dtype=np.float32))
    b1 = np.asarray(L0_ffn_b1, dtype=np.float32)
    w2 = np.ascontiguousarray(np.asarray(L0_ffn_w2, dtype=np.float32))
    b2 = np.asarray(L0_ffn_b2, dtype=np.float32)

    bf = ml_dtypes.bfloat16
    x_fm = np.ascontiguousarray(
        x.transpose(2, 0, 1).reshape(IC, 128, B * T).transpose(1, 0, 2))
    wp_b = np.ascontiguousarray(
        wp.reshape(IC, 128, IC, 128).transpose(3, 0, 2, 1)
        .reshape(128, IC, 2048))
    w1_b = np.ascontiguousarray(
        w1.reshape(2, 32, 128, IC, 128).transpose(4, 0, 1, 3, 2)
        .reshape(128, 2, 32, 2048))
    w2_b = np.ascontiguousarray(
        w2.reshape(IC, 128, 2, 2, 16, 128).transpose(5, 2, 0, 3, 4, 1)
        .reshape(128, 2, IC, 2, 2048))
    b1s = np.ascontiguousarray(b1.reshape(64, 128).T)
    b2s = np.ascontiguousarray(b2.reshape(16, 128).T)
    cosE, sinS = _rope_tables()
    qc = cosE.astype(bf)
    qs = sinS.astype(bf)
    kc = cosE.astype(bf)
    ks = sinS.astype(bf)
    mask = np.zeros((128, 4, 512), np.float32)
    ktv = np.arange(128)[:, None]
    qtv = np.arange(512)[None, :]
    for p in range(4):
        mask[:, p, :] = np.where(qtv >= p * 128 + ktv, 0.0, -1e9)
    mask = np.ascontiguousarray(mask.reshape(128, 2048)).astype(bf)
    permm = np.zeros((128, 128), bf)
    for i in range(128):
        permm[i, i ^ 1] = 1
    idn = np.eye(128, dtype=np.float32).astype(bf)
    onesc = np.ones((1, 128), bf)
    onesk = np.ones((128, 1), bf)
    onesc32 = np.ones((1, 128), np.float32)
    onesk32 = np.ones((128, 1), np.float32)
    sel8 = np.zeros((8, 2), np.float32)
    sel8[0:4, 0] = 1.0
    sel8[4:8, 1] = 1.0

    def _sc_block(g):
        if g < 64:
            return w1_b[:, g // 32, g % 32, :]
        g2 = g - 64
        return w2_b[:, g2 // 32, (g2 % 32) // 2, g2 % 2, :]

    wsc_cores = [np.ascontiguousarray(np.stack(
        [_sc_block(16 * c + i) for i in range(16)], axis=1)) for c in range(NC)]

    shared = {
        "x_fm": x_fm, "wp_i": wp_b, "w1_i": w1_b, "w2_i": w2_b,
        "b1s_i": b1s, "b2s_i": b2s, "qc_i": qc, "qs_i": qs, "kc_i": kc,
        "ks_i": ks, "mask_i": mask, "perm_i": permm, "idn_i": idn,
        "onesc_i": onesc,
        "onesk_i": onesk, "onesc32_i": onesc32, "onesk32_i": onesk32,
        "sel8_i": sel8,
    }
    in_maps = []
    for c in range(NC):
        def _qkv(base):
            return np.ascontiguousarray(
                wat[base + (2 * c) * 128: base + (2 * c + 2) * 128, :]
                .reshape(2, 128, IC, 128).transpose(3, 2, 0, 1)
                .reshape(128, 4096))
        wq_c = _qkv(0)
        wk_c = _qkv(D)
        wv_c = _qkv(2 * D)
        x_lc = np.ascontiguousarray(x_fm[:, :, c * TOKL:(c + 1) * TOKL])
        in_maps.append(dict(shared, wq_i=wq_c, wk_i=wk_c, wv_i=wv_c,
                            x_loc=x_lc, wsc_i=wsc_cores[c]))
    return in_maps


def assemble(blks):
    out = np.empty((B, T, D), np.float32)
    for c in range(NC):
        blk = np.asarray(blks[c], dtype=np.float32)
        gt0 = c * TOKL
        out[gt0 // T, gt0 % T: gt0 % T + TOKL, :] = blk.T
    return out


def kernel(L0_attn_x, L0_attn_w_atten, L0_attn_w_proj, L0_ffn_w1, L0_ffn_b1,
           L0_ffn_w2, L0_ffn_b2):
    in_maps = prepare_in_maps(L0_attn_x, L0_attn_w_atten, L0_attn_w_proj,
                              L0_ffn_w1, L0_ffn_b1, L0_ffn_w2, L0_ffn_b2)
    nc = build_nc()
    res = run_bass_kernel_spmd(nc, in_maps, core_ids=list(range(NC)))
    return assemble([res.results[c]["out_fm"] for c in range(NC)])


# version 15 (final)
# speedup vs baseline: 1.0799x; 1.0213x over previous
import numpy as np
import ml_dtypes
from contextlib import ExitStack

import concourse.bass as bass
import concourse.mybir as mybir
from concourse.bass_utils import run_bass_kernel_spmd
from concourse.aot_env import allow_ici_all_to_all

F32 = mybir.dt.float32
BF16 = mybir.dt.bfloat16
AF = mybir.ActivationFunctionType
ALU = mybir.AluOpType
AX = mybir.AxisListType

NC = 8
B, T, D = 2, 4096, 2048
H, HD = 16, 128
IC = D // 128
FF = 8192
TOKL = 1024
ROPE_BASE = 10000.0
S128 = float(np.float32(1.0) / np.sqrt(np.float32(128.0)))
XT = 256
NXB = (B * T) // XT

ENGS = ("sync", "gpsimd", "scalar", "vector", "tensor")


class _Sem:
    __slots__ = ("name", "n", "h")

    def __init__(self, name):
        self.name = name
        self.n = 0
        self.h = None


class _Sched:
    def __init__(self):
        self.ops = {e: [] for e in ENGS}
        self.sems = []

    def sem(self, name):
        s = _Sem(name)
        self.sems.append(s)
        return s

    def wait(self, eng, s, val=None):
        v = s.n if val is None else val
        if v > 0:
            self.ops[eng].append(("w", s, v))

    def emit(self, eng, fn, inc=None, k=1):
        self.ops[eng].append(("i", fn, inc, k))
        if inc is not None:
            inc.n += k
            return (inc, inc.n)
        return None


def build_nc(debug_taps=False):
    nc = bass.Bass("TRN2", target_bir_lowering=False)

    dr = lambda n, s, d, k: nc.dram_tensor(n, s, d, kind=k).ap()
    x_fm = dr("x_fm", [128, IC, B * T], F32, "ExternalInput")
    x_loc = dr("x_loc", [128, IC, TOKL], F32, "ExternalInput")
    wq_i = dr("wq_i", [128, 4096], F32, "ExternalInput")
    wk_i = dr("wk_i", [128, 4096], F32, "ExternalInput")
    wv_i = dr("wv_i", [128, 4096], F32, "ExternalInput")
    wp_i = dr("wp_i", [128, IC, 2048], F32, "ExternalInput")
    w1_i = dr("w1_i", [128, 2, 32, 2048], F32, "ExternalInput")
    w2_i = dr("w2_i", [128, 2, IC, 2, 2048], F32, "ExternalInput")
    b1s_i = dr("b1s_i", [128, 64], F32, "ExternalInput")
    b2s_i = dr("b2s_i", [128, 16], F32, "ExternalInput")
    wsc_i = dr("wsc_i", [128, 16, 2048], F32, "ExternalInput")
    sel8_i = dr("sel8_i", [8, 2], F32, "ExternalInput")
    qc_i = dr("qc_i", [128, T], BF16, "ExternalInput")
    qs_i = dr("qs_i", [128, T], BF16, "ExternalInput")
    kc_i = dr("kc_i", [128, T], BF16, "ExternalInput")
    ks_i = dr("ks_i", [128, T], BF16, "ExternalInput")
    mask_i = dr("mask_i", [128, 4 * 512], BF16, "ExternalInput")
    perm_i = dr("perm_i", [128, 128], BF16, "ExternalInput")
    idn_i = dr("idn_i", [128, 128], BF16, "ExternalInput")
    onesc_i = dr("onesc_i", [1, 128], BF16, "ExternalInput")
    onesk_i = dr("onesk_i", [128, 1], BF16, "ExternalInput")
    onesc32_i = dr("onesc32_i", [1, 128], F32, "ExternalInput")
    onesk32_i = dr("onesk32_i", [128, 1], F32, "ExternalInput")
    out_fm = dr("out_fm", [D, TOKL], F32, "ExternalOutput")

    a2a_in = [dr(f"a2a_in{hl}", [NC * 128, TOKL], BF16, "Internal")
              for hl in range(2)]
    a2a_out = [dr(f"a2a_out{hl}", [NC * 128, TOKL], BF16, "Internal")
               for hl in range(2)]
    a2a_sc_in = dr("a2a_sc_in", [NC, 128], F32, "Internal")
    a2a_sc_out = dr("a2a_sc_out", [NC, 128], F32, "Internal")
    if debug_taps:
        dbg_h = dr("dbg_h", [128, 32768], BF16, "Internal")
        dbg_acc = dr("dbg_acc", [128, 16384], F32, "Internal")

    sc = _Sched()
    LD = sc.sem("LD")
    LS = [sc.sem("LS0"), sc.sem("LS1")]
    LX = sc.sem("LX")
    GX = [sc.sem("GX0"), sc.sem("GX1")]
    GY = [sc.sem("GY0"), sc.sem("GY1")]
    GW = sc.sem("GW")
    GP = [sc.sem("GP0"), sc.sem("GP1")]
    GF = sc.sem("GF")
    GQ = sc.sem("GQ")
    SX = sc.sem("SX")
    CCS = sc.sem("CCS")
    CC = sc.sem("CC")
    CC2 = sc.sem("CC2")
    PE = sc.sem("PE")
    AC = sc.sem("AC")
    DV = sc.sem("DV")
    ST = sc.sem("ST")

    with ExitStack() as ctx:
        sb = lambda n, s, d: ctx.enter_context(nc.sbuf_tensor(n, s, d))
        R0 = sb("R0", [128, 8192], BF16)
        R1 = sb("R1", [128, 4096], F32)
        R2 = sb("R2", [128, 16384], F32)
        R3 = sb("R3", [128, 49152], BF16)
        MK = sb("MK", [128, 2048], BF16)
        ROWS = sb("ROWS", [1, 2048], F32)
        PERM = sb("PERM", [128, 128], BF16)
        IDN = sb("IDN", [128, 128], BF16)
        ONESC = sb("ONESC", [1, 128], BF16)
        ONESK = sb("ONESK", [128, 1], BF16)
        ONESC32 = sb("ONESC32", [1, 128], F32)
        ONESK32 = sb("ONESK32", [128, 1], F32)
        B1S = sb("B1S", [128, 64], F32)
        B2S = sb("B2S", [128, 16], F32)
        THR = sb("THR", [128, 4], F32)
        SEL8 = sb("SEL8", [8, 2], F32)
        SCG = sb("SCG", [8, 128], F32)
        WSUM = sb("WSUM", [128, 128], F32)
        pb = [ctx.enter_context(nc.psum_tensor(f"pb{i}", [128, 512], F32))
              for i in range(8)]
        for s in sc.sems:
            s.h = ctx.enter_context(nc.semaphore(s.name))
        block = ctx.enter_context(nc.Block())

        R2b = R2[:].bitcast(BF16)

        QCT = R2b[:, 0:4096]
        QST = R2b[:, 4096:8192]
        KCT = R2b[:, 8192:12288]
        KST = R2b[:, 12288:16384]
        XBLK = [R2b[:, 16384:20480], R2b[:, 20480:24576]]
        WVBF = R2b[:, 24576:28672]
        RAWQ = R2b[:, 28672:28928]
        RAWK = R2b[:, 28928:29184]
        TSIN = R2[:, 14848:15104]
        TCOS = R2[:, 15104:15360]
        RLB = [R2[:, 1024:1536], R2[:, 1536:2048]]
        WQBF = R0[:, 0:4096]
        WKBF = R0[:, 4096:8192]
        PBUF6 = [R0[:, i * 512:(i + 1) * 512] for i in range(6)]
        YOUT = [R0[:, 3072 + i * 512: 3584 + i * 512] for i in range(2)]
        WPBF = [R0[:, i * 2048:(i + 1) * 2048] for i in range(2)]
        SGA2 = [R0[:, 0:2048], R0[:, 2048:4096]]
        WQT = [R0[:, 4096:6144], R0[:, 6144:8192]]
        WS32 = [R1[:, 0:2048], R1[:, 2048:4096]]
        MB_T = [R1[:, 0:512], R1[:, 1024:1536]]
        RB_T = [R1[:, 512:1024], R1[:, 1536:2048]]
        qb = R3[:, 0:16384]
        kb = R3[:, 16384:32768]
        vb = R3[:, 32768:49152]
        y_fm = R3[:, 0:16384]
        rbf = R3[:, 16384:32768]
        x1b = R3[:, 32768:49152]
        sqb = R3[:, 0:16384]
        h_all = R3[:, 0:32768]
        rbf2 = R3[:, 0:16384]
        sqb2 = R3[:, 16384:32768]
        RLROW = [ROWS[:, 0:512], ROWS[:, 512:1024]]
        LNROW = [[ROWS[:, 512 + k * 512: 1024 + k * 512] for k in range(3)],
                 [R1[0:1, 2048 + k * 512: 2560 + k * 512] for k in range(3)]]
        TINY = ROWS[:, 0:1]

        m_consts = None
        for (dst, src) in [
            (QCT, qc_i), (QST, qs_i), (KCT, kc_i), (KST, ks_i),
            (MK[:], mask_i), (PERM[:], perm_i), (IDN[:], idn_i),
            (ONESC[:], onesc_i),
            (ONESK[:], onesk_i), (ONESC32[:], onesc32_i),
            (ONESK32[:], onesk32_i), (B1S[:], b1s_i), (B2S[:], b2s_i),
            (SEL8[:], sel8_i),
        ]:
            m_consts = sc.emit("sync", lambda e, d=dst, s=src: e.dma_start(out=d, in_=s), LD, 16)
        sc.emit("gpsimd", lambda e: e.dma_start(out=WQBF, in_=wq_i), GW, 16)
        sc.emit("gpsimd", lambda e: e.dma_start(out=WKBF, in_=wk_i), GW, 16)
        m_wqkv = sc.emit("gpsimd", lambda e: e.dma_start(out=WVBF, in_=wv_i), GW, 16)

        N_SC_BLK = 16

        sblk = {"i": 0}
        wstream_free = [None, None]

        def emit_scale_blocks(kk):
            for _ in range(kk):
                i = sblk["i"]
                if i >= N_SC_BLK:
                    return
                bi = i % 2
                wsl = WS32[bi]
                if wstream_free[bi] is not None:
                    sc.wait("sync", *wstream_free[bi])
                src = wsc_i[:, i, :]
                for ph in range(2):
                    m_d = sc.emit("sync", lambda e, dst=wsl, s=src, ph=ph: e.dma_start(
                        out=dst[:, ph * 1024:(ph + 1) * 1024],
                        in_=s[:, ph * 1024:(ph + 1) * 1024]), LS[bi], 16)
                sc.wait("vector", LS[bi], m_d[1])
                col = 4 * i
                for ch in range(4):
                    sc.emit("vector", lambda e, wsl=wsl, c=col + ch, ch=ch: e.tensor_reduce(
                        out=WSUM[:, c:c + 1], in_=wsl[:, ch * 512:(ch + 1) * 512],
                        axis=AX.X, op=ALU.add, apply_absolute_value=True), DV)
                wstream_free[bi] = (DV, DV.n)
                sblk["i"] += 1

        m_xblk = [None] * NXB
        xblk_free = [None, None]
        sc.wait("tensor", GW, m_wqkv[1])
        sc.wait("tensor", LD, m_consts[1])

        m_rawfree = {"q": None, "k": None}
        m_qkps_free = {"q": None, "k": None}
        m_permfree = [None, None]
        m_vps_free = [None] * 4
        last_rope_dve = None

        for tb in range(NXB):
            bi = tb % 2
            b = tb // 16
            pos0 = (tb % 16) * XT
            if xblk_free[bi] is not None:
                sc.wait("gpsimd", *xblk_free[bi])
            m_xblk[tb] = sc.emit("gpsimd", lambda e, bi=bi, tb=tb: e.dma_start(
                out=XBLK[bi].rearrange("p (ic t) -> p ic t", ic=IC),
                in_=x_fm[:, :, tb * XT:(tb + 1) * XT]),
                GX[bi], 16)
            sc.wait("tensor", GX[bi], m_xblk[tb][1])
            for hl in range(2):
                bh = hl * 2 + b
                for which, wbf, psl, raw, ctab, stab, dstq in (
                    ("q", WQBF, pb[0], RAWQ, QCT, QST, qb),
                    ("k", WKBF, pb[1], RAWK, KCT, KST, kb),
                ):
                    if m_qkps_free[which] is not None:
                        sc.wait("tensor", *m_qkps_free[which])
                    m_mm = None
                    for ic in range(IC):
                        r = sc.emit("tensor", lambda e, w=wbf, ic=ic, hl=hl, bi=bi, ps=psl: e.matmul(
                            ps[:, 0:XT],
                            w[:, (ic * 2 + hl) * 128:(ic * 2 + hl + 1) * 128],
                            XBLK[bi][:, ic * XT:(ic + 1) * XT],
                            start=(ic == 0), stop=(ic == IC - 1)),
                            PE if ic == IC - 1 else None)
                        if r is not None:
                            m_mm = r
                    if m_rawfree[which] is not None:
                        sc.wait("scalar", *m_rawfree[which])
                    sc.wait("scalar", PE, m_mm[1])
                    m_raw = sc.emit("scalar", lambda e, ps=psl, raw=raw: e.activation(
                        out=raw, in_=ps[:, 0:XT], func=AF.Copy), AC)
                    m_qkps_free[which] = (AC, m_raw[1])
                    if m_permfree[hl] is not None:
                        sc.wait("tensor", *m_permfree[hl])
                    sc.wait("tensor", AC, m_raw[1])
                    m_sw = sc.emit("tensor", lambda e, raw=raw, hl=hl: e.matmul(
                        pb[2 + hl][:, 0:XT], PERM[:], raw, start=True, stop=True), PE)
                    sc.wait("vector", PE, m_sw[1])
                    if last_rope_dve is not None:
                        sc.wait("vector", DV, last_rope_dve[1])
                    sc.emit("vector", lambda e, hl=hl, st=stab, pos0=pos0: e.tensor_tensor(
                        out=TSIN, in0=pb[2 + hl][:, 0:XT], in1=st[:, pos0:pos0 + XT], op=ALU.mult), DV)
                    m_tc = sc.emit("vector", lambda e, raw=raw, ct=ctab, pos0=pos0: e.tensor_tensor(
                        out=TCOS, in0=raw, in1=ct[:, pos0:pos0 + XT], op=ALU.mult), DV)
                    sc.wait("vector", DV, m_tc[1])
                    m_rot = sc.emit("vector", lambda e, dq=dstq, bh=bh, pos0=pos0: e.tensor_tensor(
                        out=dq[:, bh * 4096 + pos0: bh * 4096 + pos0 + XT],
                        in0=TSIN, in1=TCOS, op=ALU.add), DV)
                    last_rope_dve = m_rot
                    m_rawfree[which] = (DV, m_rot[1])
                    m_permfree[hl] = (DV, m_rot[1])
                if hl == 0:
                    for s_ in range(2):
                        vslot = (tb * 2 + s_) % 4
                        psv = pb[4 + vslot]
                        if m_vps_free[vslot] is not None:
                            sc.wait("tensor", *m_vps_free[vslot])
                        m_mv = None
                        for ic in range(IC):
                            r = sc.emit("tensor", lambda e, s_=s_, ic=ic, bi=bi, psv=psv: e.matmul(
                                psv[:, 0:256],
                                XBLK[bi][:, ic * XT + s_ * 128: ic * XT + s_ * 128 + 128],
                                WVBF[:, ic * 256:(ic + 1) * 256],
                                start=(ic == 0), stop=(ic == IC - 1)),
                                PE if ic == IC - 1 else None)
                            if r is not None:
                                m_mv = r
                        sc.wait("scalar", PE, m_mv[1])
                        jsub = (tb % 16) * 2 + s_
                        m_ve = None
                        for hl2 in range(2):
                            bh2 = hl2 * 2 + b
                            m_ve = sc.emit("scalar", lambda e, psv=psv, bh2=bh2, jsub=jsub, hl2=hl2: e.activation(
                                out=vb[:, bh2 * 4096 + jsub * 128: bh2 * 4096 + jsub * 128 + 128],
                                in_=psv[:, hl2 * 128:(hl2 + 1) * 128], func=AF.Copy), AC)
                        m_vps_free[vslot] = (AC, m_ve[1])
            xblk_free[bi] = (PE, PE.n)
            emit_scale_blocks(2)

        m_ps = sc.emit("vector", lambda e: e.tensor_reduce(
            out=WSUM[:, 64:65], in_=WSUM[:, 0:64], axis=AX.X, op=ALU.add), DV)
        sc.wait("tensor", DV, m_ps[1])
        m_pmm = sc.emit("tensor", lambda e: e.matmul(
            pb[7][0:1, 0:1], ONESK32[:], WSUM[:, 64:65], start=True, stop=True), PE)
        sc.wait("scalar", PE, m_pmm[1])
        m_pc = sc.emit("scalar", lambda e: e.activation(
            out=WSUM[0:1, 65:66], in_=pb[7][0:1, 0:1], func=AF.Copy), AC)
        sc.wait("sync", AC, m_pc[1])
        m_sd = None
        for dcore in range(NC):
            m_sd = sc.emit("sync", lambda e, d=dcore: e.dma_start(
                out=a2a_sc_in[d:d + 1, 0:1], in_=WSUM[0:1, 65:66]), SX, 16)
        sc.wait("gpsimd", SX, m_sd[1])

        def _ccs(e):
            with allow_ici_all_to_all():
                return e.collective_compute(
                    "AllToAll", ALU.bypass,
                    replica_groups=[list(range(NC))],
                    ins=[a2a_sc_in.opt()], outs=[a2a_sc_out.opt()])

        sc.emit("gpsimd", _ccs, CCS)

        sc.wait("tensor", DV, DV.n)
        sc.wait("tensor", AC, AC.n)

        LOOKA = 3
        exp_done = [None] * 4
        slot_free = [None] * 6
        yp_free = [None, None]
        rl_free = [None, None]
        rlb_free = [None, None]
        yout_free = [None, None]
        rl_inuse = [None, None]
        pb7_free = [(AC, m_pc[1])]
        pend = []
        fin_pend = []

        y_fm_hv = y_fm.rearrange("p (s two t) -> p s two t", s=8, two=2)

        def emit_half_exchange(hl):
            sc.wait("gpsimd", GY[0], GY[0].n)
            sc.wait("gpsimd", GY[1], GY[1].n)

            def _cc(e, hl=hl):
                with allow_ici_all_to_all():
                    return e.collective_compute(
                        "AllToAll", ALU.bypass,
                        replica_groups=[list(range(NC))],
                        ins=[a2a_in[hl].opt()], outs=[a2a_out[hl].opt()])

            sc.emit("gpsimd", _cc, CC if hl == 0 else CC2)

        def fin_stage(f):
            par = f["par"]
            st = f["stage"]
            if st == 0:
                rl = RLROW[par]
                sc.wait("vector", PE, f["lstop"])
                if rl_inuse[par] is not None:
                    sc.wait("vector", *rl_inuse[par])
                f["m_r"] = sc.emit("vector", lambda e, par=par, rl=rl: e.reciprocal(
                    out=rl, in_=pb[6][64 * par:64 * par + 1, :]), DV)
                rl_free[par] = (DV, f["m_r"][1])
            elif st == 1:
                rl = RLROW[par]
                if pb7_free[0] is not None:
                    sc.wait("tensor", *pb7_free[0])
                sc.wait("tensor", DV, f["m_r"][1])
                f["m_bc"] = sc.emit("tensor", lambda e, rl=rl: e.matmul(
                    pb[7][:], ONESC32[:], rl, start=True, stop=True), PE)
                rl_inuse[par] = (PE, f["m_bc"][1])
            elif st == 2:
                if rlb_free[par] is not None:
                    sc.wait("scalar", *rlb_free[par])
                sc.wait("scalar", PE, f["m_bc"][1])
                f["m_cp"] = sc.emit("scalar", lambda e, par=par: e.activation(
                    out=RLB[par], in_=pb[7][:], func=AF.Copy), AC)
                pb7_free[0] = (AC, f["m_cp"][1])
            else:
                if yout_free[par] is not None:
                    sc.wait("vector", *yout_free[par])
                sc.wait("vector", AC, f["m_cp"][1])
                m_yo = sc.emit("vector", lambda e, par=par, ypb=f["ypb"]: e.tensor_tensor(
                    out=YOUT[par], in0=ypb[:], in1=RLB[par], op=ALU.mult), DV)
                yp_free[par] = (DV, m_yo[1])
                rlb_free[par] = (DV, m_yo[1])
                sc.wait("sync", DV, m_yo[1])
                r0 = (f["b"] * 4 + f["qtb"] // 2) * 128
                c0 = (f["qtb"] % 2) * 512
                m_st = sc.emit("sync", lambda e, r0=r0, c0=c0, par=par, hl=f["hl"]: e.dma_start(
                    out=a2a_in[hl][r0:r0 + 128, c0:c0 + 512], in_=YOUT[par]), GY[par], 16)
                yout_free[par] = (GY[par], m_st[1])
                if f["hl"] == 0 and f["b"] == 1 and f["qtb"] == 7:
                    emit_half_exchange(0)
                emit_scale_blocks(4)
            f["stage"] += 1

        def pop_one():
            r = pend.pop(0)
            while fin_pend:
                f0 = fin_pend[0]
                if f0["stage"] < 4 and f0["age"] >= f0["stage"] + 1:
                    fin_stage(f0)
                    if f0["stage"] == 4:
                        fin_pend.pop(0)
                        continue
                break
            for f in fin_pend:
                f["age"] += 1
            par = r["par"]
            sc.wait("tensor", AC, r["expv"])
            if r["j"] == 0:
                if yp_free[par] is not None:
                    sc.wait("tensor", *yp_free[par])
                if rl_free[par] is not None:
                    sc.wait("tensor", *rl_free[par])
            sc.emit("tensor", lambda e, r=r: e.matmul(
                r["ypb"][:],
                vb[:, r["bh"] * 4096 + r["j"] * 128: r["bh"] * 4096 + (r["j"] + 1) * 128],
                PBUF6[r["sl"]], start=(r["j"] == 0), stop=(r["j"] == r["n"] - 1)), None)
            m_l = sc.emit("tensor", lambda e, r=r: e.matmul(
                pb[6][64 * r["par"]:64 * r["par"] + 1, :], ONESK[:], PBUF6[r["sl"]],
                start=(r["j"] == 0), stop=(r["j"] == r["n"] - 1)), PE)
            slot_free[r["sl"]] = (PE, m_l[1])
            if r["j"] == r["n"] - 1:
                fin_pend.append(dict(age=0, stage=0, par=par, ypb=r["ypb"],
                                     lstop=m_l[1], hl=r["hl"], b=r["b"],
                                     qtb=r["qtb"]))

        gj = 0
        for (hl, b) in [(0, 0), (0, 1), (1, 0), (1, 1)]:
            bh = hl * 2 + b
            for qtb in range(8):
                n = 4 * qtb + 4
                par = qtb % 2
                ypb = pb[4 + par]
                for j in range(n):
                    sb_i = gj % 4
                    sl_i = gj % 6
                    sp = pb[sb_i]
                    diag = j >= n - 4
                    if exp_done[sb_i] is not None:
                        sc.wait("tensor", *exp_done[sb_i])
                    m_s = sc.emit("tensor", lambda e, sp=sp, bh=bh, j=j, qtb=qtb, diag=diag: e.matmul(
                        sp[:], kb[:, bh * 4096 + j * 128: bh * 4096 + (j + 1) * 128],
                        qb[:, bh * 4096 + qtb * 512: bh * 4096 + (qtb + 1) * 512],
                        start=True, stop=(not diag)), None if diag else PE)
                    if diag:
                        p_ = j - (n - 4)
                        m_s = sc.emit("tensor", lambda e, sp=sp, p_=p_: e.matmul(
                            sp[:], IDN[:], MK[:, p_ * 512:(p_ + 1) * 512],
                            start=False, stop=True), PE)
                    if slot_free[sl_i] is not None:
                        sc.wait("scalar", *slot_free[sl_i])
                    sc.wait("scalar", PE, m_s[1])
                    m_e = sc.emit("scalar", lambda e, sp=sp, sl_i=sl_i: e.activation(
                        out=PBUF6[sl_i], in_=sp[:], func=AF.Exp, scale=S128), AC)
                    exp_done[sb_i] = (AC, m_e[1])
                    pend.append(dict(sl=sl_i, expv=m_e[1], par=par, bh=bh,
                                     j=j, n=n, ypb=ypb, hl=hl, b=b, qtb=qtb))
                    gj += 1
                    if len(pend) > LOOKA:
                        pop_one()
        while pend:
            pop_one()
        while fin_pend:
            f0 = fin_pend.pop(0)
            while f0["stage"] < 4:
                fin_stage(f0)
        emit_scale_blocks(N_SC_BLK)

        emit_half_exchange(1)
        sc.wait("gpsimd", CC, 1)
        sc.wait("gpsimd", CC2, 1)
        m_yfm = None
        for hl2 in range(2):
            m_yfm = sc.emit("gpsimd", lambda e, hl2=hl2: e.dma_start(
                out=y_fm_hv[:, :, hl2, :],
                in_=a2a_out[hl2].rearrange("(s p) t -> p s t", p=128)), GF, 16)

        sc.wait("gpsimd", CCS, 1)
        m_scg = sc.emit("gpsimd", lambda e: e.dma_start(
            out=SCG[:], in_=a2a_sc_out), GQ, 16)
        sc.wait("tensor", GQ, m_scg[1])
        sc.wait("tensor", AC, AC.n)
        m_sel = None
        for gcol in range(2):
            m_sel = sc.emit("tensor", lambda e, g=gcol: e.matmul(
                pb[7][0:1, g:g + 1], SEL8[:, g:g + 1], SCG[:, 0:1],
                start=True, stop=True), PE)
        m_cp2 = None
        for (gcol, cth) in [(0, 0), (1, 2)]:
            if m_cp2 is not None:
                sc.wait("tensor", AC, m_cp2[1])
            sc.wait("scalar", PE, m_sel[1])
            sc.wait("scalar", AC, AC.n)
            m_cp = sc.emit("scalar", lambda e, g=gcol: e.activation(
                out=TINY, in_=pb[7][0:1, g:g + 1], func=AF.Copy,
                scale=0.5 / (FF * D)), AC)
            sc.wait("tensor", AC, m_cp[1])
            m_mm2 = sc.emit("tensor", lambda e: e.matmul(
                pb[7][:, 2:3], ONESC32[:], TINY, start=True, stop=True), PE)
            sc.wait("scalar", PE, m_mm2[1])
            m_cp2 = sc.emit("scalar", lambda e, ct=cth: e.activation(
                out=THR[:, ct + 1:ct + 2], in_=pb[7][:, 2:3], func=AF.Copy), AC)
            sc.wait("vector", AC, m_cp2[1])
            sc.emit("vector", lambda e, ct=cth: e.tensor_scalar(
                out=THR[:, ct:ct + 1], in0=THR[:, ct + 1:ct + 2],
                scalar1=-1.0, scalar2=None, op0=ALU.mult), DV)

        sc.wait("sync", DV, DV.n)
        m_xloc = sc.emit("sync", lambda e: e.dma_start(
            out=R2[:].rearrange("p (a t) -> p a t", a=IC),
            in_=x_loc), LX, 16)
        m_wp = [None] * IC
        sc.wait("gpsimd", DV, DV.n)

        def emit_wp(oc):
            bi = oc % 2
            m_wp[oc] = sc.emit("gpsimd", lambda e, oc=oc, bi=bi: e.dma_start(
                out=WPBF[bi], in_=wp_i[:, oc, :]), GP[bi], 16)

        emit_wp(0)
        emit_wp(1)
        sc.wait("tensor", GF, m_yfm[1])
        sc.wait("vector", LX, m_xloc[1])
        m_prev_evac = [None, None]
        for oc in range(IC):
            bi = oc % 2
            sc.wait("tensor", GP[bi], m_wp[oc][1])
            for th in range(2):
                if m_prev_evac[th] is not None:
                    sc.wait("tensor", *m_prev_evac[th])
            m_p = [None, None]
            for ic in range(IC):
                for th in range(2):
                    r = sc.emit("tensor", lambda e, bi=bi, ic=ic, th=th: e.matmul(
                        pb[th][:], WPBF[bi][:, ic * 128:(ic + 1) * 128],
                        y_fm[:, ic * 1024 + th * 512: ic * 1024 + th * 512 + 512],
                        start=(ic == 0), stop=(ic == IC - 1)),
                        PE if ic == IC - 1 else None)
                    if r is not None:
                        m_p[th] = r
            for th in range(2):
                sc.wait("vector", PE, m_p[th][1])
                col = oc * 1024 + th * 512
                m_r1 = sc.emit("vector", lambda e, th=th, col=col: e.tensor_tensor(
                    out=R2[:, col:col + 512], in0=pb[th][:], in1=R2[:, col:col + 512],
                    op=ALU.add), DV)
                m_prev_evac[th] = (DV, m_r1[1])
            if oc + 2 < IC:
                sc.wait("gpsimd", PE, PE.n)
                emit_wp(oc + 2)

        LN_SUMBK = ((2, 3), (6, 7))
        LN_BCBK = ((4, 5), (0, 1))

        out_fm_v = out_fm.rearrange("(a p) t -> p a t", p=128)

        def layer_norm(cast_b, sq_b, out_bf, store_out=False):
            sc.wait("scalar", DV, DV.n)
            sc.wait("scalar", AC, AC.n)
            sc.wait("scalar", PE, PE.n)
            sc.wait("tensor", AC, AC.n)
            sc.wait("tensor", DV, DV.n)
            R2v = R2[:].rearrange("p (a t) -> p a t", a=IC)
            cbv = cast_b.rearrange("p (a t) -> p a t", a=IC)
            sqv = sq_b.rearrange("p (a t) -> p a t", a=IC)
            m_sq = {}
            for th in range(2):
                sl = slice(th * 512, th * 512 + 512)
                sc.emit("scalar", lambda e, sl=sl: e.activation(
                    out=cbv[:, :, sl], in_=R2v[:, :, sl], func=AF.Copy), AC)
                m_sq[th] = sc.emit("scalar", lambda e, sl=sl: e.activation(
                    out=sqv[:, :, sl], in_=R2v[:, :, sl], func=AF.Square), AC)
            m_sums = {}
            for th in range(2):
                b1, b2 = LN_SUMBK[th]
                sc.wait("tensor", AC, m_sq[th][1])
                for (bk, src) in ((b1, cast_b), (b2, sq_b)):
                    m = None
                    for ic in range(IC):
                        r = sc.emit("tensor", lambda e, ic=ic, th=th, bk=bk, src=src: e.matmul(
                            pb[bk][0:1, :], ONESK[:],
                            src[:, ic * 1024 + th * 512: ic * 1024 + th * 512 + 512],
                            start=(ic == 0), stop=(ic == IC - 1)), PE if ic == IC - 1 else None)
                        if r is not None:
                            m = r
                m_sums[th] = m
            m_rb = {}
            for th in range(2):
                b1, b2 = LN_SUMBK[th]
                c1, c2 = LN_BCBK[th]
                mrow, qrow, rsrow = LNROW[th]
                sc.wait("scalar", PE, m_sums[th][1])
                sc.emit("scalar", lambda e, b1=b1, mrow=mrow: e.activation(
                    out=mrow, in_=pb[b1][0:1, :], func=AF.Copy, scale=1.0 / D), AC)
                m_q = sc.emit("scalar", lambda e, b2=b2, qrow=qrow: e.activation(
                    out=qrow, in_=pb[b2][0:1, :], func=AF.Copy, scale=1.0 / D), AC)
                sc.wait("vector", AC, m_q[1])
                m_m2 = sc.emit("vector", lambda e, mrow=mrow, rsrow=rsrow: e.tensor_tensor(
                    out=rsrow, in0=mrow, in1=mrow, op=ALU.mult), DV)
                sc.wait("vector", DV, m_m2[1])
                m_var = sc.emit("vector", lambda e, qrow=qrow, rsrow=rsrow: e.scalar_tensor_tensor(
                    out=rsrow, in0=qrow, scalar=1e-5, in1=rsrow,
                    op0=ALU.add, op1=ALU.subtract), DV)
                sc.wait("scalar", DV, m_var[1])
                m_sd = sc.emit("scalar", lambda e, rsrow=rsrow: e.activation(
                    out=rsrow, in_=rsrow, func=AF.Sqrt), AC)
                sc.wait("vector", AC, m_sd[1])
                m_rs = sc.emit("vector", lambda e, rsrow=rsrow: e.reciprocal(
                    out=rsrow, in_=rsrow), DV)
                sc.wait("tensor", DV, m_rs[1])
                sc.emit("tensor", lambda e, c1=c1, mrow=mrow: e.matmul(
                    pb[c1][:], ONESC32[:], mrow, start=True, stop=True), PE)
                m_br = sc.emit("tensor", lambda e, c2=c2, rsrow=rsrow: e.matmul(
                    pb[c2][:], ONESC32[:], rsrow, start=True, stop=True), PE)
                m_rb[th] = m_br
            mlast = None
            for th in range(2):
                c1, c2 = LN_BCBK[th]
                sc.wait("vector", PE, m_rb[th][1])
                for ic in range(IC):
                    col = ic * 1024 + th * 512
                    m_sub = sc.emit("vector", lambda e, col=col, c1=c1: e.tensor_tensor(
                        out=R2[:, col:col + 512], in0=R2[:, col:col + 512], in1=pb[c1][:],
                        op=ALU.subtract), DV)
                    sc.wait("vector", DV, m_sub[1])
                    if out_bf is not None:
                        mlast = sc.emit("vector", lambda e, col=col, c2=c2, d=out_bf: e.tensor_tensor(
                            out=d[:, col:col + 512], in0=R2[:, col:col + 512], in1=pb[c2][:],
                            op=ALU.mult), DV)
                    else:
                        mlast = sc.emit("vector", lambda e, col=col, c2=c2: e.tensor_tensor(
                            out=R2[:, col:col + 512], in0=R2[:, col:col + 512], in1=pb[c2][:],
                            op=ALU.mult), DV)
                        if store_out:
                            sc.wait("sync", DV, mlast[1])
                            sc.emit("sync", lambda e, ic=ic, th=th, col=col: e.dma_start(
                                out=out_fm_v[:, ic, th * 512:(th + 1) * 512],
                                in_=R2[:, col:col + 512]), ST, 16)
            sc.wait("scalar", DV, mlast[1])
            sc.wait("tensor", DV, mlast[1])
            return mlast

        sga_free = [None, None]

        def quant_block(src_f32, dst_bf, thrN_col, thrP_col, dst_free, bi):
            sga = SGA2[bi]
            if sga_free[bi] is not None:
                sc.wait("scalar", *sga_free[bi])
            sc.emit("scalar", lambda e, s=src_f32, c=thrN_col, sga=sga: e.activation(
                out=sga, in_=s, func=AF.Sign, bias=THR[:, c:c + 1], scale=1.0), AC)
            if dst_free is not None:
                sc.wait("scalar", *dst_free)
            m_b = sc.emit("scalar", lambda e, s=src_f32, c=thrP_col, d=dst_bf: e.activation(
                out=d, in_=s, func=AF.Sign, bias=THR[:, c:c + 1], scale=1.0), AC)
            sc.wait("vector", AC, m_b[1])
            m_w = sc.emit("vector", lambda e, d=dst_bf, sga=sga: e.tensor_tensor(
                out=d, in0=sga, in1=d, op=ALU.add), DV)
            sga_free[bi] = (DV, m_w[1])
            return (DV, m_w[1]), (AC, m_b[1])

        w1_free = list(wstream_free)
        wqt_free = [None, None]
        m_hev = {}
        m_accinit = None

        def ffn1_quant(half, hc):
            bi = hc % 2
            q = "sync" if bi == 0 else "gpsimd"
            wsl = WS32[bi]
            if w1_free[bi] is not None:
                sc.wait(q, *w1_free[bi])
            for ph in range(4):
                m_wd = sc.emit(q, lambda e, half=half, hc=hc, dst=wsl, ph=ph: e.dma_start(
                    out=dst[:, ph * 512:(ph + 1) * 512],
                    in_=w1_i[:, half, hc, ph * 512:(ph + 1) * 512]), LS[bi], 16)
            sc.wait("scalar", LS[bi], m_wd[1])
            m_wq, m_ws = quant_block(wsl, WQT[bi], 0, 1, wqt_free[bi], bi)
            w1_free[bi] = m_ws
            return m_wq

        sc.wait("scalar", PE, PE.n)
        sc.wait("scalar", DV, DV.n)
        m_wq0 = ffn1_quant(0, 0)

        m_x1 = layer_norm(rbf, sqb, x1b)
        sc.wait("sync", PE, PE.n)
        sc.wait("gpsimd", PE, PE.n)

        for half in range(2):
            m_wq_cur = m_wq0 if half == 0 else ffn1_quant(half, 0)
            for hc in range(32):
                bi = hc % 2
                hcG = half * 32 + hc
                if half == 0 and hc == 0:
                    sc.wait("tensor", DV, m_x1[1])
                sc.wait("tensor", *m_wq_cur)
                for th in range(2):
                    if m_hev.get((bi, th)) is not None:
                        sc.wait("tensor", *m_hev[(bi, th)])
                m_h = [None, None]
                for ic in range(IC):
                    for th in range(2):
                        p_ = 2 * bi + th
                        r = sc.emit("tensor", lambda e, bi=bi, ic=ic, th=th, p_=p_: e.matmul(
                            pb[p_][:], WQT[bi][:, ic * 128:(ic + 1) * 128],
                            x1b[:, ic * 1024 + th * 512: ic * 1024 + th * 512 + 512],
                            start=(ic == 0), stop=(ic == IC - 1)),
                            PE if ic == IC - 1 else None)
                        if r is not None:
                            m_h[th] = r
                wqt_free[bi] = (PE, PE.n)
                if hc + 1 < 32:
                    m_wq_cur = ffn1_quant(half, hc + 1)
                for th in range(2):
                    p_ = 2 * bi + th
                    sc.wait("scalar", PE, m_h[th][1])
                    m_g = sc.emit("scalar", lambda e, hc=hc, th=th, hcG=hcG, p_=p_: e.activation(
                        out=h_all[:, hc * 1024 + th * 512: hc * 1024 + th * 512 + 512],
                        in_=pb[p_][:], func=AF.Gelu,
                        bias=B1S[:, hcG:hcG + 1], scale=0.5), AC)
                    m_hev[(bi, th)] = (AC, m_g[1])
            if debug_taps and half == 0:
                sc.wait("sync", AC, AC.n)
                sc.emit("sync", lambda e: e.dma_start(out=dbg_h, in_=h_all), ST, 16)
            if half == 0:
                sc.wait("vector", DV, m_x1[1])
                for oc in range(IC):
                    m_accinit = sc.emit("vector", lambda e, oc=oc: e.tensor_scalar(
                        out=R2[:, oc * 1024:(oc + 1) * 1024],
                        in0=x1b[:, oc * 1024:(oc + 1) * 1024],
                        scalar1=B2S[:, oc:oc + 1], scalar2=None, op0=ALU.add), DV)
            sc.wait("tensor", AC, AC.n)
            sc.wait("vector", DV, DV.n)
            w2_free = list(w1_free)
            m_f2evac = [None, None]

            def ffn2_quant(half, oc, hs):
                bi = hs
                q = "sync" if bi == 0 else "gpsimd"
                wsl = WS32[bi]
                if w2_free[bi] is not None:
                    sc.wait(q, *w2_free[bi])
                for ph in range(4):
                    m_wd = sc.emit(q, lambda e, half=half, oc=oc, hs=hs, dst=wsl, ph=ph: e.dma_start(
                        out=dst[:, ph * 512:(ph + 1) * 512],
                        in_=w2_i[:, half, oc, hs, ph * 512:(ph + 1) * 512]), LS[bi], 16)
                sc.wait("scalar", LS[bi], m_wd[1])
                mq, ms = quant_block(wsl, WQT[bi], 2, 3, wqt_free[bi], bi)
                w2_free[bi] = ms
                wqt_free[bi] = None
                return mq

            m_wq2 = [ffn2_quant(half, 0, 0), ffn2_quant(half, 0, 1)]
            for oc in range(IC):
                for th in range(2):
                    if m_f2evac[th] is not None:
                        sc.wait("tensor", *m_f2evac[th])
                m_f = [None, None]
                m_wq2_next = [None, None]
                for hs in range(2):
                    sc.wait("tensor", *m_wq2[hs])
                    for icc in range(16):
                        hcc = hs * 16 + icc
                        for th in range(2):
                            r = sc.emit("tensor", lambda e, hs=hs, icc=icc, th=th, hcc=hcc: e.matmul(
                                pb[4 + th][:], WQT[hs][:, icc * 128:(icc + 1) * 128],
                                h_all[:, hcc * 1024 + th * 512: hcc * 1024 + th * 512 + 512],
                                start=(hcc == 0), stop=(hcc == 31)),
                                PE if icc == 15 else None)
                            if r is not None and hcc == 31:
                                m_f[th] = r
                    wqt_free[hs] = (PE, PE.n)
                    if oc + 1 < IC:
                        m_wq2_next[hs] = ffn2_quant(half, oc + 1, hs)
                for th in range(2):
                    sc.wait("vector", PE, m_f[th][1])
                    if half == 0 and oc == 0 and th == 0:
                        sc.wait("vector", DV, m_accinit[1])
                    col = oc * 1024 + th * 512
                    m_a2 = sc.emit("vector", lambda e, th=th, col=col: e.scalar_tensor_tensor(
                        out=R2[:, col:col + 512], in0=pb[4 + th][:], scalar=0.5,
                        in1=R2[:, col:col + 512], op0=ALU.mult, op1=ALU.add), DV)
                    m_f2evac[th] = (DV, m_a2[1])
                if oc + 1 < IC:
                    m_wq2 = m_wq2_next
            w1_free = list(w2_free)
            if debug_taps and half == 0:
                sc.wait("sync", DV, DV.n)
                sc.emit("sync", lambda e: e.dma_start(out=dbg_acc, in_=R2[:]), ST, 16)

        layer_norm(rbf2, sqb2, None, store_out=True)
        sc.wait("sync", ST, ST.n)

        def run_ops(e, eng):
            for op in sc.ops[eng]:
                if op[0] == "w":
                    _, s, v = op
                    e.wait_ge(s.h, v)
                else:
                    _, fn, inc, k = op
                    inst = fn(e)
                    if inc is not None:
                        inst.then_inc(inc.h, k)

        @block.sync
        def _(e):
            run_ops(e, "sync")

        @block.gpsimd
        def _(e):
            run_ops(e, "gpsimd")

        @block.scalar
        def _(e):
            run_ops(e, "scalar")

        @block.vector
        def _(e):
            run_ops(e, "vector")

        @block.tensor
        def _(e):
            run_ops(e, "tensor")

    return nc


def _rope_tables():
    theta = 1.0 / ROPE_BASE ** (np.arange(0, HD, 2, dtype=np.float32) / HD)
    t = np.arange(T, dtype=np.float32)
    ang = np.outer(theta.astype(np.float64), t.astype(np.float64))
    cos = np.cos(ang).astype(np.float32)
    sin = np.sin(ang).astype(np.float32)
    cosE = np.repeat(cos, 2, axis=0)
    sinS = np.empty((HD, T), np.float32)
    sinS[0::2] = -sin
    sinS[1::2] = sin
    return cosE, sinS


def prepare_in_maps(L0_attn_x, L0_attn_w_atten, L0_attn_w_proj, L0_ffn_w1,
                    L0_ffn_b1, L0_ffn_w2, L0_ffn_b2):
    x = np.ascontiguousarray(np.asarray(L0_attn_x, dtype=np.float32))
    wat = np.ascontiguousarray(np.asarray(L0_attn_w_atten, dtype=np.float32))
    wp = np.ascontiguousarray(np.asarray(L0_attn_w_proj, dtype=np.float32))
    w1 = np.ascontiguousarray(np.asarray(L0_ffn_w1, dtype=np.float32))
    b1 = np.asarray(L0_ffn_b1, dtype=np.float32)
    w2 = np.ascontiguousarray(np.asarray(L0_ffn_w2, dtype=np.float32))
    b2 = np.asarray(L0_ffn_b2, dtype=np.float32)

    bf = ml_dtypes.bfloat16
    x_fm = np.ascontiguousarray(
        x.transpose(2, 0, 1).reshape(IC, 128, B * T).transpose(1, 0, 2))
    wp_b = np.ascontiguousarray(
        wp.reshape(IC, 128, IC, 128).transpose(3, 0, 2, 1)
        .reshape(128, IC, 2048))
    w1_b = np.ascontiguousarray(
        w1.reshape(2, 32, 128, IC, 128).transpose(4, 0, 1, 3, 2)
        .reshape(128, 2, 32, 2048))
    w2_b = np.ascontiguousarray(
        w2.reshape(IC, 128, 2, 2, 16, 128).transpose(5, 2, 0, 3, 4, 1)
        .reshape(128, 2, IC, 2, 2048))
    b1s = np.ascontiguousarray(b1.reshape(64, 128).T)
    b2s = np.ascontiguousarray(b2.reshape(16, 128).T)
    cosE, sinS = _rope_tables()
    qc = cosE.astype(bf)
    qs = sinS.astype(bf)
    kc = cosE.astype(bf)
    ks = sinS.astype(bf)
    mask = np.zeros((128, 4, 512), np.float32)
    ktv = np.arange(128)[:, None]
    qtv = np.arange(512)[None, :]
    for p in range(4):
        mask[:, p, :] = np.where(qtv >= p * 128 + ktv, 0.0, -1e9)
    mask = np.ascontiguousarray(mask.reshape(128, 2048)).astype(bf)
    permm = np.zeros((128, 128), bf)
    for i in range(128):
        permm[i, i ^ 1] = 1
    idn = np.eye(128, dtype=np.float32).astype(bf)
    onesc = np.ones((1, 128), bf)
    onesk = np.ones((128, 1), bf)
    onesc32 = np.ones((1, 128), np.float32)
    onesk32 = np.ones((128, 1), np.float32)
    sel8 = np.zeros((8, 2), np.float32)
    sel8[0:4, 0] = 1.0
    sel8[4:8, 1] = 1.0

    def _sc_block(g):
        if g < 64:
            return w1_b[:, g // 32, g % 32, :]
        g2 = g - 64
        return w2_b[:, g2 // 32, (g2 % 32) // 2, g2 % 2, :]

    wsc_cores = [np.ascontiguousarray(np.stack(
        [_sc_block(16 * c + i) for i in range(16)], axis=1)) for c in range(NC)]

    shared = {
        "x_fm": x_fm, "wp_i": wp_b, "w1_i": w1_b, "w2_i": w2_b,
        "b1s_i": b1s, "b2s_i": b2s, "qc_i": qc, "qs_i": qs, "kc_i": kc,
        "ks_i": ks, "mask_i": mask, "perm_i": permm, "idn_i": idn,
        "onesc_i": onesc,
        "onesk_i": onesk, "onesc32_i": onesc32, "onesk32_i": onesk32,
        "sel8_i": sel8,
    }
    in_maps = []
    for c in range(NC):
        def _qkv(base):
            return np.ascontiguousarray(
                wat[base + (2 * c) * 128: base + (2 * c + 2) * 128, :]
                .reshape(2, 128, IC, 128).transpose(3, 2, 0, 1)
                .reshape(128, 4096))
        wq_c = _qkv(0)
        wk_c = _qkv(D)
        wv_c = _qkv(2 * D)
        x_lc = np.ascontiguousarray(x_fm[:, :, c * TOKL:(c + 1) * TOKL])
        in_maps.append(dict(shared, wq_i=wq_c, wk_i=wk_c, wv_i=wv_c,
                            x_loc=x_lc, wsc_i=wsc_cores[c]))
    return in_maps


def assemble(blks):
    out = np.empty((B, T, D), np.float32)
    for c in range(NC):
        blk = np.asarray(blks[c], dtype=np.float32)
        gt0 = c * TOKL
        out[gt0 // T, gt0 % T: gt0 % T + TOKL, :] = blk.T
    return out


def kernel(L0_attn_x, L0_attn_w_atten, L0_attn_w_proj, L0_ffn_w1, L0_ffn_b1,
           L0_ffn_w2, L0_ffn_b2):
    in_maps = prepare_in_maps(L0_attn_x, L0_attn_w_atten, L0_attn_w_proj,
                              L0_ffn_w1, L0_ffn_b1, L0_ffn_w2, L0_ffn_b2)
    nc = build_nc()
    res = run_bass_kernel_spmd(nc, in_maps, core_ids=list(range(NC)))
    return assemble([res.results[c]["out_fm"] for c in range(NC)])


# version 16
# speedup vs baseline: 1.0836x; 1.0034x over previous
import numpy as np
import ml_dtypes
from contextlib import ExitStack

import concourse.bass as bass
import concourse.mybir as mybir
from concourse.bass_utils import run_bass_kernel_spmd
from concourse.aot_env import allow_ici_all_to_all

F32 = mybir.dt.float32
BF16 = mybir.dt.bfloat16
AF = mybir.ActivationFunctionType
ALU = mybir.AluOpType
AX = mybir.AxisListType

NC = 8
B, T, D = 2, 4096, 2048
H, HD = 16, 128
IC = D // 128
FF = 8192
TOKL = 1024
ROPE_BASE = 10000.0
S128 = float(np.float32(1.0) / np.sqrt(np.float32(128.0)))
XT = 256
NXB = (B * T) // XT

ENGS = ("sync", "gpsimd", "scalar", "vector", "tensor")


class _Sem:
    __slots__ = ("name", "n", "h")

    def __init__(self, name):
        self.name = name
        self.n = 0
        self.h = None


class _Sched:
    def __init__(self):
        self.ops = {e: [] for e in ENGS}
        self.sems = []

    def sem(self, name):
        s = _Sem(name)
        self.sems.append(s)
        return s

    def wait(self, eng, s, val=None):
        v = s.n if val is None else val
        if v > 0:
            self.ops[eng].append(("w", s, v))

    def emit(self, eng, fn, inc=None, k=1):
        self.ops[eng].append(("i", fn, inc, k))
        if inc is not None:
            inc.n += k
            return (inc, inc.n)
        return None


def build_nc(debug_taps=False):
    nc = bass.Bass("TRN2", target_bir_lowering=False)

    dr = lambda n, s, d, k: nc.dram_tensor(n, s, d, kind=k).ap()
    x_fm = dr("x_fm", [128, IC, B * T], F32, "ExternalInput")
    x_loc = dr("x_loc", [128, IC, TOKL], F32, "ExternalInput")
    wq_i = dr("wq_i", [128, 4096], F32, "ExternalInput")
    wk_i = dr("wk_i", [128, 4096], F32, "ExternalInput")
    wv_i = dr("wv_i", [128, 4096], F32, "ExternalInput")
    wp_i = dr("wp_i", [128, IC, 2048], F32, "ExternalInput")
    w1_i = dr("w1_i", [128, 2, 32, 2048], F32, "ExternalInput")
    w2_i = dr("w2_i", [128, 2, IC, 2, 2048], F32, "ExternalInput")
    b1s_i = dr("b1s_i", [128, 64], F32, "ExternalInput")
    b2s_i = dr("b2s_i", [128, 16], F32, "ExternalInput")
    wsc_i = dr("wsc_i", [128, 16, 2048], F32, "ExternalInput")
    sel8_i = dr("sel8_i", [8, 2], F32, "ExternalInput")
    qc_i = dr("qc_i", [128, T], BF16, "ExternalInput")
    qs_i = dr("qs_i", [128, T], BF16, "ExternalInput")
    kc_i = dr("kc_i", [128, T], BF16, "ExternalInput")
    ks_i = dr("ks_i", [128, T], BF16, "ExternalInput")
    mask_i = dr("mask_i", [128, 4 * 512], BF16, "ExternalInput")
    perm_i = dr("perm_i", [128, 128], BF16, "ExternalInput")
    idn_i = dr("idn_i", [128, 128], BF16, "ExternalInput")
    onesc_i = dr("onesc_i", [1, 128], BF16, "ExternalInput")
    onesk_i = dr("onesk_i", [128, 1], BF16, "ExternalInput")
    onesc32_i = dr("onesc32_i", [1, 128], F32, "ExternalInput")
    onesk32_i = dr("onesk32_i", [128, 1], F32, "ExternalInput")
    out_fm = dr("out_fm", [D, TOKL], F32, "ExternalOutput")

    a2a_in = [dr(f"a2a_in{hl}", [NC * 128, TOKL], BF16, "Internal")
              for hl in range(2)]
    a2a_out = [dr(f"a2a_out{hl}", [NC * 128, TOKL], BF16, "Internal")
               for hl in range(2)]
    a2a_sc_in = dr("a2a_sc_in", [NC, 128], F32, "Internal")
    a2a_sc_out = dr("a2a_sc_out", [NC, 128], F32, "Internal")
    if debug_taps:
        dbg_h = dr("dbg_h", [128, 32768], BF16, "Internal")
        dbg_acc = dr("dbg_acc", [128, 16384], F32, "Internal")

    sc = _Sched()
    LD = sc.sem("LD")
    LS = [sc.sem("LS0"), sc.sem("LS1")]
    LX = sc.sem("LX")
    GX = [sc.sem("GX0"), sc.sem("GX1")]
    GY = [sc.sem("GY0"), sc.sem("GY1")]
    GW = sc.sem("GW")
    GP = [sc.sem("GP0"), sc.sem("GP1")]
    GF = sc.sem("GF")
    GQ = sc.sem("GQ")
    SX = sc.sem("SX")
    CCS = sc.sem("CCS")
    CC = sc.sem("CC")
    CC2 = sc.sem("CC2")
    PE = sc.sem("PE")
    AC = sc.sem("AC")
    DV = sc.sem("DV")
    ST = sc.sem("ST")

    with ExitStack() as ctx:
        sb = lambda n, s, d: ctx.enter_context(nc.sbuf_tensor(n, s, d))
        R0 = sb("R0", [128, 8192], BF16)
        R1 = sb("R1", [128, 4096], F32)
        R2 = sb("R2", [128, 16384], F32)
        R3 = sb("R3", [128, 49152], BF16)
        MK = sb("MK", [128, 2048], BF16)
        ROWS = sb("ROWS", [1, 2048], F32)
        PERM = sb("PERM", [128, 128], BF16)
        IDN = sb("IDN", [128, 128], BF16)
        ONESC = sb("ONESC", [1, 128], BF16)
        ONESK = sb("ONESK", [128, 1], BF16)
        ONESC32 = sb("ONESC32", [1, 128], F32)
        ONESK32 = sb("ONESK32", [128, 1], F32)
        B1S = sb("B1S", [128, 64], F32)
        B2S = sb("B2S", [128, 16], F32)
        THR = sb("THR", [128, 4], F32)
        SEL8 = sb("SEL8", [8, 2], F32)
        SCG = sb("SCG", [8, 128], F32)
        WSUM = sb("WSUM", [128, 128], F32)
        pb = [ctx.enter_context(nc.psum_tensor(f"pb{i}", [128, 512], F32))
              for i in range(8)]
        for s in sc.sems:
            s.h = ctx.enter_context(nc.semaphore(s.name))
        block = ctx.enter_context(nc.Block())

        R2b = R2[:].bitcast(BF16)

        QCT = R2b[:, 0:4096]
        QST = R2b[:, 4096:8192]
        KCT = R2b[:, 8192:12288]
        KST = R2b[:, 12288:16384]
        XBLK = [R2b[:, 16384:20480], R2b[:, 20480:24576]]
        WVBF = R2b[:, 24576:28672]
        RAWQ = R2b[:, 28672:28928]
        RAWK = R2b[:, 28928:29184]
        TSIN = R2[:, 14848:15104]
        TCOS = R2[:, 15104:15360]
        RLB = [R2[:, 1024:1536], R2[:, 1536:2048]]
        WQBF = R0[:, 0:4096]
        WKBF = R0[:, 4096:8192]
        PBUF6 = [R0[:, i * 512:(i + 1) * 512] for i in range(6)]
        YOUT = [R0[:, 3072 + i * 512: 3584 + i * 512] for i in range(2)]
        WPBF = [R0[:, i * 2048:(i + 1) * 2048] for i in range(2)]
        SGA2 = [R0[:, 0:2048], R0[:, 2048:4096]]
        WQT = [R0[:, 4096:6144], R0[:, 6144:8192]]
        WS32 = [R1[:, 0:2048], R1[:, 2048:4096]]
        MB_T = [R1[:, 0:512], R1[:, 1024:1536]]
        RB_T = [R1[:, 512:1024], R1[:, 1536:2048]]
        qb = R3[:, 0:16384]
        kb = R3[:, 16384:32768]
        vb = R3[:, 32768:49152]
        y_fm = R3[:, 0:16384]
        rbf = R3[:, 16384:32768]
        x1b = R3[:, 32768:49152]
        sqb = R3[:, 0:16384]
        h_all = R3[:, 0:32768]
        rbf2 = R3[:, 0:16384]
        sqb2 = R3[:, 16384:32768]
        RLROW = [ROWS[:, 0:512], ROWS[:, 512:1024]]
        LNROW = [[ROWS[:, 512 + k * 512: 1024 + k * 512] for k in range(3)],
                 [R1[0:1, 2048 + k * 512: 2560 + k * 512] for k in range(3)]]
        TINY = ROWS[:, 0:1]

        m_consts = None
        for (dst, src) in [
            (QCT, qc_i), (QST, qs_i), (KCT, kc_i), (KST, ks_i),
            (MK[:], mask_i), (PERM[:], perm_i), (IDN[:], idn_i),
            (ONESC[:], onesc_i),
            (ONESK[:], onesk_i), (ONESC32[:], onesc32_i),
            (ONESK32[:], onesk32_i), (B1S[:], b1s_i), (B2S[:], b2s_i),
            (SEL8[:], sel8_i),
        ]:
            m_consts = sc.emit("sync", lambda e, d=dst, s=src: e.dma_start(out=d, in_=s), LD, 16)
        sc.emit("gpsimd", lambda e: e.dma_start(out=WQBF, in_=wq_i), GW, 16)
        sc.emit("gpsimd", lambda e: e.dma_start(out=WKBF, in_=wk_i), GW, 16)
        m_wqkv = sc.emit("gpsimd", lambda e: e.dma_start(out=WVBF, in_=wv_i), GW, 16)

        N_SC_BLK = 16

        sblk = {"i": 0}
        wstream_free = [None, None]

        def emit_scale_blocks(kk):
            for _ in range(kk):
                i = sblk["i"]
                if i >= N_SC_BLK:
                    return
                bi = i % 2
                wsl = WS32[bi]
                if wstream_free[bi] is not None:
                    sc.wait("sync", *wstream_free[bi])
                src = wsc_i[:, i, :]
                for ph in range(2):
                    m_d = sc.emit("sync", lambda e, dst=wsl, s=src, ph=ph: e.dma_start(
                        out=dst[:, ph * 1024:(ph + 1) * 1024],
                        in_=s[:, ph * 1024:(ph + 1) * 1024]), LS[bi], 16)
                sc.wait("vector", LS[bi], m_d[1])
                col = 4 * i
                for ch in range(4):
                    sc.emit("vector", lambda e, wsl=wsl, c=col + ch, ch=ch: e.tensor_reduce(
                        out=WSUM[:, c:c + 1], in_=wsl[:, ch * 512:(ch + 1) * 512],
                        axis=AX.X, op=ALU.add, apply_absolute_value=True), DV)
                wstream_free[bi] = (DV, DV.n)
                sblk["i"] += 1

        m_xblk = [None] * NXB
        xblk_free = [None, None]
        sc.wait("tensor", GW, m_wqkv[1])
        sc.wait("tensor", LD, m_consts[1])

        m_rawfree = {"q": None, "k": None}
        m_qkps_free = {"q": None, "k": None}
        m_permfree = [None, None]
        m_vps_free = [None] * 4
        last_rope_dve = None

        for tb in range(NXB):
            bi = tb % 2
            b = tb // 16
            pos0 = (tb % 16) * XT
            if xblk_free[bi] is not None:
                sc.wait("gpsimd", *xblk_free[bi])
            m_xblk[tb] = sc.emit("gpsimd", lambda e, bi=bi, tb=tb: e.dma_start(
                out=XBLK[bi].rearrange("p (ic t) -> p ic t", ic=IC),
                in_=x_fm[:, :, tb * XT:(tb + 1) * XT]),
                GX[bi], 16)
            sc.wait("tensor", GX[bi], m_xblk[tb][1])
            for hl in range(2):
                bh = hl * 2 + b
                for which, wbf, psl, raw, ctab, stab, dstq in (
                    ("q", WQBF, pb[0], RAWQ, QCT, QST, qb),
                    ("k", WKBF, pb[1], RAWK, KCT, KST, kb),
                ):
                    if m_qkps_free[which] is not None:
                        sc.wait("tensor", *m_qkps_free[which])
                    m_mm = None
                    for ic in range(IC):
                        r = sc.emit("tensor", lambda e, w=wbf, ic=ic, hl=hl, bi=bi, ps=psl: e.matmul(
                            ps[:, 0:XT],
                            w[:, (ic * 2 + hl) * 128:(ic * 2 + hl + 1) * 128],
                            XBLK[bi][:, ic * XT:(ic + 1) * XT],
                            start=(ic == 0), stop=(ic == IC - 1)),
                            PE if ic == IC - 1 else None)
                        if r is not None:
                            m_mm = r
                    if m_rawfree[which] is not None:
                        sc.wait("scalar", *m_rawfree[which])
                    sc.wait("scalar", PE, m_mm[1])
                    m_raw = sc.emit("scalar", lambda e, ps=psl, raw=raw: e.activation(
                        out=raw, in_=ps[:, 0:XT], func=AF.Copy), AC)
                    m_qkps_free[which] = (AC, m_raw[1])
                    if m_permfree[hl] is not None:
                        sc.wait("tensor", *m_permfree[hl])
                    sc.wait("tensor", AC, m_raw[1])
                    m_sw = sc.emit("tensor", lambda e, raw=raw, hl=hl: e.matmul(
                        pb[2 + hl][:, 0:XT], PERM[:], raw, start=True, stop=True), PE)
                    sc.wait("vector", PE, m_sw[1])
                    if last_rope_dve is not None:
                        sc.wait("vector", DV, last_rope_dve[1])
                    sc.emit("vector", lambda e, hl=hl, st=stab, pos0=pos0: e.tensor_tensor(
                        out=TSIN, in0=pb[2 + hl][:, 0:XT], in1=st[:, pos0:pos0 + XT], op=ALU.mult), DV)
                    m_tc = sc.emit("vector", lambda e, raw=raw, ct=ctab, pos0=pos0: e.tensor_tensor(
                        out=TCOS, in0=raw, in1=ct[:, pos0:pos0 + XT], op=ALU.mult), DV)
                    sc.wait("vector", DV, m_tc[1])
                    m_rot = sc.emit("vector", lambda e, dq=dstq, bh=bh, pos0=pos0: e.tensor_tensor(
                        out=dq[:, bh * 4096 + pos0: bh * 4096 + pos0 + XT],
                        in0=TSIN, in1=TCOS, op=ALU.add), DV)
                    last_rope_dve = m_rot
                    m_rawfree[which] = (DV, m_rot[1])
                    m_permfree[hl] = (DV, m_rot[1])
                if hl == 0:
                    for s_ in range(2):
                        vslot = (tb * 2 + s_) % 4
                        psv = pb[4 + vslot]
                        if m_vps_free[vslot] is not None:
                            sc.wait("tensor", *m_vps_free[vslot])
                        m_mv = None
                        for ic in range(IC):
                            r = sc.emit("tensor", lambda e, s_=s_, ic=ic, bi=bi, psv=psv: e.matmul(
                                psv[:, 0:256],
                                XBLK[bi][:, ic * XT + s_ * 128: ic * XT + s_ * 128 + 128],
                                WVBF[:, ic * 256:(ic + 1) * 256],
                                start=(ic == 0), stop=(ic == IC - 1)),
                                PE if ic == IC - 1 else None)
                            if r is not None:
                                m_mv = r
                        sc.wait("scalar", PE, m_mv[1])
                        jsub = (tb % 16) * 2 + s_
                        m_ve = None
                        for hl2 in range(2):
                            bh2 = hl2 * 2 + b
                            m_ve = sc.emit("scalar", lambda e, psv=psv, bh2=bh2, jsub=jsub, hl2=hl2: e.activation(
                                out=vb[:, bh2 * 4096 + jsub * 128: bh2 * 4096 + jsub * 128 + 128],
                                in_=psv[:, hl2 * 128:(hl2 + 1) * 128], func=AF.Copy), AC)
                        m_vps_free[vslot] = (AC, m_ve[1])
            xblk_free[bi] = (PE, PE.n)
            emit_scale_blocks(2)

        m_ps = sc.emit("vector", lambda e: e.tensor_reduce(
            out=WSUM[:, 64:65], in_=WSUM[:, 0:64], axis=AX.X, op=ALU.add), DV)
        sc.wait("tensor", DV, m_ps[1])
        m_pmm = sc.emit("tensor", lambda e: e.matmul(
            pb[7][0:1, 0:1], ONESK32[:], WSUM[:, 64:65], start=True, stop=True), PE)
        sc.wait("scalar", PE, m_pmm[1])
        m_pc = sc.emit("scalar", lambda e: e.activation(
            out=WSUM[0:1, 65:66], in_=pb[7][0:1, 0:1], func=AF.Copy), AC)
        sc.wait("sync", AC, m_pc[1])
        m_sd = None
        for dcore in range(NC):
            m_sd = sc.emit("sync", lambda e, d=dcore: e.dma_start(
                out=a2a_sc_in[d:d + 1, 0:1], in_=WSUM[0:1, 65:66]), SX, 16)
        sc.wait("gpsimd", SX, m_sd[1])

        def _ccs(e):
            with allow_ici_all_to_all():
                return e.collective_compute(
                    "AllToAll", ALU.bypass,
                    replica_groups=[list(range(NC))],
                    ins=[a2a_sc_in.opt()], outs=[a2a_sc_out.opt()])

        sc.emit("gpsimd", _ccs, CCS)

        sc.wait("tensor", DV, DV.n)
        sc.wait("tensor", AC, AC.n)

        LOOKA = 5
        exp_done = [None] * 4
        slot_free = [None] * 6
        yp_free = [None, None]
        rl_free = [None, None]
        rlb_free = [None, None]
        yout_free = [None, None]
        rl_inuse = [None, None]
        pb7_free = [(AC, m_pc[1])]
        pend = []
        fin_pend = []

        y_fm_hv = y_fm.rearrange("p (s two t) -> p s two t", s=8, two=2)

        def emit_half_exchange(hl):
            sc.wait("gpsimd", GY[0], GY[0].n)
            sc.wait("gpsimd", GY[1], GY[1].n)

            def _cc(e, hl=hl):
                with allow_ici_all_to_all():
                    return e.collective_compute(
                        "AllToAll", ALU.bypass,
                        replica_groups=[list(range(NC))],
                        ins=[a2a_in[hl].opt()], outs=[a2a_out[hl].opt()])

            sc.emit("gpsimd", _cc, CC if hl == 0 else CC2)

        def fin_stage(f):
            par = f["par"]
            st = f["stage"]
            if st == 0:
                rl = RLROW[par]
                sc.wait("vector", PE, f["lstop"])
                if rl_inuse[par] is not None:
                    sc.wait("vector", *rl_inuse[par])
                f["m_r"] = sc.emit("vector", lambda e, par=par, rl=rl: e.reciprocal(
                    out=rl, in_=pb[6][64 * par:64 * par + 1, :]), DV)
                rl_free[par] = (DV, f["m_r"][1])
            elif st == 1:
                rl = RLROW[par]
                if pb7_free[0] is not None:
                    sc.wait("tensor", *pb7_free[0])
                sc.wait("tensor", DV, f["m_r"][1])
                f["m_bc"] = sc.emit("tensor", lambda e, rl=rl: e.matmul(
                    pb[7][:], ONESC32[:], rl, start=True, stop=True), PE)
                rl_inuse[par] = (PE, f["m_bc"][1])
            elif st == 2:
                if rlb_free[par] is not None:
                    sc.wait("scalar", *rlb_free[par])
                sc.wait("scalar", PE, f["m_bc"][1])
                f["m_cp"] = sc.emit("scalar", lambda e, par=par: e.activation(
                    out=RLB[par], in_=pb[7][:], func=AF.Copy), AC)
                pb7_free[0] = (AC, f["m_cp"][1])
            else:
                if yout_free[par] is not None:
                    sc.wait("vector", *yout_free[par])
                sc.wait("vector", AC, f["m_cp"][1])
                m_yo = sc.emit("vector", lambda e, par=par, ypb=f["ypb"]: e.tensor_tensor(
                    out=YOUT[par], in0=ypb[:], in1=RLB[par], op=ALU.mult), DV)
                yp_free[par] = (DV, m_yo[1])
                rlb_free[par] = (DV, m_yo[1])
                sc.wait("sync", DV, m_yo[1])
                r0 = (f["b"] * 4 + f["qtb"] // 2) * 128
                c0 = (f["qtb"] % 2) * 512
                m_st = sc.emit("sync", lambda e, r0=r0, c0=c0, par=par, hl=f["hl"]: e.dma_start(
                    out=a2a_in[hl][r0:r0 + 128, c0:c0 + 512], in_=YOUT[par]), GY[par], 16)
                yout_free[par] = (GY[par], m_st[1])
                if f["hl"] == 0 and f["b"] == 1 and f["qtb"] == 7:
                    emit_half_exchange(0)
                emit_scale_blocks(4)
            f["stage"] += 1

        def pop_one():
            r = pend.pop(0)
            while fin_pend:
                f0 = fin_pend[0]
                if f0["stage"] < 4 and f0["age"] >= f0["stage"] + 1:
                    fin_stage(f0)
                    if f0["stage"] == 4:
                        fin_pend.pop(0)
                        continue
                break
            for f in fin_pend:
                f["age"] += 1
            par = r["par"]
            sc.wait("tensor", AC, r["expv"])
            if r["j"] == 0:
                if yp_free[par] is not None:
                    sc.wait("tensor", *yp_free[par])
                if rl_free[par] is not None:
                    sc.wait("tensor", *rl_free[par])
            sc.emit("tensor", lambda e, r=r: e.matmul(
                r["ypb"][:],
                vb[:, r["bh"] * 4096 + r["j"] * 128: r["bh"] * 4096 + (r["j"] + 1) * 128],
                PBUF6[r["sl"]], start=(r["j"] == 0), stop=(r["j"] == r["n"] - 1)), None)
            m_l = sc.emit("tensor", lambda e, r=r: e.matmul(
                pb[6][64 * r["par"]:64 * r["par"] + 1, :], ONESK[:], PBUF6[r["sl"]],
                start=(r["j"] == 0), stop=(r["j"] == r["n"] - 1)), PE)
            slot_free[r["sl"]] = (PE, m_l[1])
            if r["j"] == r["n"] - 1:
                fin_pend.append(dict(age=0, stage=0, par=par, ypb=r["ypb"],
                                     lstop=m_l[1], hl=r["hl"], b=r["b"],
                                     qtb=r["qtb"]))

        gj = 0
        for (hl, b) in [(0, 0), (0, 1), (1, 0), (1, 1)]:
            bh = hl * 2 + b
            for qtb in range(8):
                n = 4 * qtb + 4
                par = qtb % 2
                ypb = pb[4 + par]
                for j in range(n):
                    sb_i = gj % 4
                    sl_i = gj % 6
                    sp = pb[sb_i]
                    diag = j >= n - 4
                    if exp_done[sb_i] is not None:
                        sc.wait("tensor", *exp_done[sb_i])
                    m_s = sc.emit("tensor", lambda e, sp=sp, bh=bh, j=j, qtb=qtb, diag=diag: e.matmul(
                        sp[:], kb[:, bh * 4096 + j * 128: bh * 4096 + (j + 1) * 128],
                        qb[:, bh * 4096 + qtb * 512: bh * 4096 + (qtb + 1) * 512],
                        start=True, stop=(not diag)), None if diag else PE)
                    if diag:
                        p_ = j - (n - 4)
                        m_s = sc.emit("tensor", lambda e, sp=sp, p_=p_: e.matmul(
                            sp[:], IDN[:], MK[:, p_ * 512:(p_ + 1) * 512],
                            start=False, stop=True), PE)
                    if slot_free[sl_i] is not None:
                        sc.wait("scalar", *slot_free[sl_i])
                    sc.wait("scalar", PE, m_s[1])
                    m_e = sc.emit("scalar", lambda e, sp=sp, sl_i=sl_i: e.activation(
                        out=PBUF6[sl_i], in_=sp[:], func=AF.Exp, scale=S128), AC)
                    exp_done[sb_i] = (AC, m_e[1])
                    pend.append(dict(sl=sl_i, expv=m_e[1], par=par, bh=bh,
                                     j=j, n=n, ypb=ypb, hl=hl, b=b, qtb=qtb))
                    gj += 1
                    if len(pend) > LOOKA:
                        pop_one()
        while pend:
            pop_one()
        while fin_pend:
            f0 = fin_pend.pop(0)
            while f0["stage"] < 4:
                fin_stage(f0)
        emit_scale_blocks(N_SC_BLK)

        emit_half_exchange(1)
        sc.wait("gpsimd", CC, 1)
        sc.wait("gpsimd", CC2, 1)
        m_yfm = None
        for hl2 in range(2):
            m_yfm = sc.emit("gpsimd", lambda e, hl2=hl2: e.dma_start(
                out=y_fm_hv[:, :, hl2, :],
                in_=a2a_out[hl2].rearrange("(s p) t -> p s t", p=128)), GF, 16)

        sc.wait("gpsimd", CCS, 1)
        m_scg = sc.emit("gpsimd", lambda e: e.dma_start(
            out=SCG[:], in_=a2a_sc_out), GQ, 16)
        sc.wait("tensor", GQ, m_scg[1])
        sc.wait("tensor", AC, AC.n)
        m_sel = None
        for gcol in range(2):
            m_sel = sc.emit("tensor", lambda e, g=gcol: e.matmul(
                pb[7][0:1, g:g + 1], SEL8[:, g:g + 1], SCG[:, 0:1],
                start=True, stop=True), PE)
        m_cp2 = None
        for (gcol, cth) in [(0, 0), (1, 2)]:
            if m_cp2 is not None:
                sc.wait("tensor", AC, m_cp2[1])
            sc.wait("scalar", PE, m_sel[1])
            sc.wait("scalar", AC, AC.n)
            m_cp = sc.emit("scalar", lambda e, g=gcol: e.activation(
                out=TINY, in_=pb[7][0:1, g:g + 1], func=AF.Copy,
                scale=0.5 / (FF * D)), AC)
            sc.wait("tensor", AC, m_cp[1])
            m_mm2 = sc.emit("tensor", lambda e: e.matmul(
                pb[7][:, 2:3], ONESC32[:], TINY, start=True, stop=True), PE)
            sc.wait("scalar", PE, m_mm2[1])
            m_cp2 = sc.emit("scalar", lambda e, ct=cth: e.activation(
                out=THR[:, ct + 1:ct + 2], in_=pb[7][:, 2:3], func=AF.Copy), AC)
            sc.wait("vector", AC, m_cp2[1])
            sc.emit("vector", lambda e, ct=cth: e.tensor_scalar(
                out=THR[:, ct:ct + 1], in0=THR[:, ct + 1:ct + 2],
                scalar1=-1.0, scalar2=None, op0=ALU.mult), DV)

        sc.wait("sync", DV, DV.n)
        m_xloc = sc.emit("sync", lambda e: e.dma_start(
            out=R2[:].rearrange("p (a t) -> p a t", a=IC),
            in_=x_loc), LX, 16)
        m_wp = [None] * IC
        sc.wait("gpsimd", DV, DV.n)

        def emit_wp(oc):
            bi = oc % 2
            m_wp[oc] = sc.emit("gpsimd", lambda e, oc=oc, bi=bi: e.dma_start(
                out=WPBF[bi], in_=wp_i[:, oc, :]), GP[bi], 16)

        emit_wp(0)
        emit_wp(1)
        sc.wait("tensor", GF, m_yfm[1])
        sc.wait("vector", LX, m_xloc[1])
        m_prev_evac = [None, None]
        for oc in range(IC):
            bi = oc % 2
            sc.wait("tensor", GP[bi], m_wp[oc][1])
            for th in range(2):
                if m_prev_evac[th] is not None:
                    sc.wait("tensor", *m_prev_evac[th])
            m_p = [None, None]
            for ic in range(IC):
                for th in range(2):
                    r = sc.emit("tensor", lambda e, bi=bi, ic=ic, th=th: e.matmul(
                        pb[th][:], WPBF[bi][:, ic * 128:(ic + 1) * 128],
                        y_fm[:, ic * 1024 + th * 512: ic * 1024 + th * 512 + 512],
                        start=(ic == 0), stop=(ic == IC - 1)),
                        PE if ic == IC - 1 else None)
                    if r is not None:
                        m_p[th] = r
            for th in range(2):
                sc.wait("vector", PE, m_p[th][1])
                col = oc * 1024 + th * 512
                m_r1 = sc.emit("vector", lambda e, th=th, col=col: e.tensor_tensor(
                    out=R2[:, col:col + 512], in0=pb[th][:], in1=R2[:, col:col + 512],
                    op=ALU.add), DV)
                m_prev_evac[th] = (DV, m_r1[1])
            if oc + 2 < IC:
                sc.wait("gpsimd", PE, PE.n)
                emit_wp(oc + 2)

        LN_SUMBK = ((2, 3), (6, 7))
        LN_BCBK = ((4, 5), (0, 1))

        out_fm_v = out_fm.rearrange("(a p) t -> p a t", p=128)

        def layer_norm(cast_b, sq_b, out_bf, store_out=False):
            sc.wait("scalar", DV, DV.n)
            sc.wait("scalar", AC, AC.n)
            sc.wait("scalar", PE, PE.n)
            sc.wait("tensor", AC, AC.n)
            sc.wait("tensor", DV, DV.n)
            R2v = R2[:].rearrange("p (a t) -> p a t", a=IC)
            cbv = cast_b.rearrange("p (a t) -> p a t", a=IC)
            sqv = sq_b.rearrange("p (a t) -> p a t", a=IC)
            m_sq = {}
            for th in range(2):
                sl = slice(th * 512, th * 512 + 512)
                sc.emit("scalar", lambda e, sl=sl: e.activation(
                    out=cbv[:, :, sl], in_=R2v[:, :, sl], func=AF.Copy), AC)
                m_sq[th] = sc.emit("scalar", lambda e, sl=sl: e.activation(
                    out=sqv[:, :, sl], in_=R2v[:, :, sl], func=AF.Square), AC)
            m_sums = {}
            for th in range(2):
                b1, b2 = LN_SUMBK[th]
                sc.wait("tensor", AC, m_sq[th][1])
                for (bk, src) in ((b1, cast_b), (b2, sq_b)):
                    m = None
                    for ic in range(IC):
                        r = sc.emit("tensor", lambda e, ic=ic, th=th, bk=bk, src=src: e.matmul(
                            pb[bk][0:1, :], ONESK[:],
                            src[:, ic * 1024 + th * 512: ic * 1024 + th * 512 + 512],
                            start=(ic == 0), stop=(ic == IC - 1)), PE if ic == IC - 1 else None)
                        if r is not None:
                            m = r
                m_sums[th] = m
            m_rb = {}
            for th in range(2):
                b1, b2 = LN_SUMBK[th]
                c1, c2 = LN_BCBK[th]
                mrow, qrow, rsrow = LNROW[th]
                sc.wait("scalar", PE, m_sums[th][1])
                sc.emit("scalar", lambda e, b1=b1, mrow=mrow: e.activation(
                    out=mrow, in_=pb[b1][0:1, :], func=AF.Copy, scale=1.0 / D), AC)
                m_q = sc.emit("scalar", lambda e, b2=b2, qrow=qrow: e.activation(
                    out=qrow, in_=pb[b2][0:1, :], func=AF.Copy, scale=1.0 / D), AC)
                sc.wait("vector", AC, m_q[1])
                m_m2 = sc.emit("vector", lambda e, mrow=mrow, rsrow=rsrow: e.tensor_tensor(
                    out=rsrow, in0=mrow, in1=mrow, op=ALU.mult), DV)
                sc.wait("vector", DV, m_m2[1])
                m_var = sc.emit("vector", lambda e, qrow=qrow, rsrow=rsrow: e.scalar_tensor_tensor(
                    out=rsrow, in0=qrow, scalar=1e-5, in1=rsrow,
                    op0=ALU.add, op1=ALU.subtract), DV)
                sc.wait("scalar", DV, m_var[1])
                m_sd = sc.emit("scalar", lambda e, rsrow=rsrow: e.activation(
                    out=rsrow, in_=rsrow, func=AF.Sqrt), AC)
                sc.wait("vector", AC, m_sd[1])
                m_rs = sc.emit("vector", lambda e, rsrow=rsrow: e.reciprocal(
                    out=rsrow, in_=rsrow), DV)
                sc.wait("tensor", DV, m_rs[1])
                sc.emit("tensor", lambda e, c1=c1, mrow=mrow: e.matmul(
                    pb[c1][:], ONESC32[:], mrow, start=True, stop=True), PE)
                m_br = sc.emit("tensor", lambda e, c2=c2, rsrow=rsrow: e.matmul(
                    pb[c2][:], ONESC32[:], rsrow, start=True, stop=True), PE)
                m_rb[th] = m_br
            mlast = None
            for th in range(2):
                c1, c2 = LN_BCBK[th]
                sc.wait("vector", PE, m_rb[th][1])
                for ic in range(IC):
                    col = ic * 1024 + th * 512
                    m_sub = sc.emit("vector", lambda e, col=col, c1=c1: e.tensor_tensor(
                        out=R2[:, col:col + 512], in0=R2[:, col:col + 512], in1=pb[c1][:],
                        op=ALU.subtract), DV)
                    sc.wait("vector", DV, m_sub[1])
                    if out_bf is not None:
                        mlast = sc.emit("vector", lambda e, col=col, c2=c2, d=out_bf: e.tensor_tensor(
                            out=d[:, col:col + 512], in0=R2[:, col:col + 512], in1=pb[c2][:],
                            op=ALU.mult), DV)
                    else:
                        mlast = sc.emit("vector", lambda e, col=col, c2=c2: e.tensor_tensor(
                            out=R2[:, col:col + 512], in0=R2[:, col:col + 512], in1=pb[c2][:],
                            op=ALU.mult), DV)
                        if store_out:
                            sc.wait("sync", DV, mlast[1])
                            sc.emit("sync", lambda e, ic=ic, th=th, col=col: e.dma_start(
                                out=out_fm_v[:, ic, th * 512:(th + 1) * 512],
                                in_=R2[:, col:col + 512]), ST, 16)
            sc.wait("scalar", DV, mlast[1])
            sc.wait("tensor", DV, mlast[1])
            return mlast

        sga_free = [None, None]

        def quant_block(src_f32, dst_bf, thrN_col, thrP_col, dst_free, bi):
            sga = SGA2[bi]
            if sga_free[bi] is not None:
                sc.wait("scalar", *sga_free[bi])
            sc.emit("scalar", lambda e, s=src_f32, c=thrN_col, sga=sga: e.activation(
                out=sga, in_=s, func=AF.Sign, bias=THR[:, c:c + 1], scale=1.0), AC)
            if dst_free is not None:
                sc.wait("scalar", *dst_free)
            m_b = sc.emit("scalar", lambda e, s=src_f32, c=thrP_col, d=dst_bf: e.activation(
                out=d, in_=s, func=AF.Sign, bias=THR[:, c:c + 1], scale=1.0), AC)
            sc.wait("vector", AC, m_b[1])
            m_w = sc.emit("vector", lambda e, d=dst_bf, sga=sga: e.tensor_tensor(
                out=d, in0=sga, in1=d, op=ALU.add), DV)
            sga_free[bi] = (DV, m_w[1])
            return (DV, m_w[1]), (AC, m_b[1])

        w1_free = list(wstream_free)
        wqt_free = [None, None]
        m_hev = {}
        m_accinit = None

        def ffn1_quant(half, hc):
            bi = hc % 2
            q = "sync" if bi == 0 else "gpsimd"
            wsl = WS32[bi]
            if w1_free[bi] is not None:
                sc.wait(q, *w1_free[bi])
            for ph in range(4):
                m_wd = sc.emit(q, lambda e, half=half, hc=hc, dst=wsl, ph=ph: e.dma_start(
                    out=dst[:, ph * 512:(ph + 1) * 512],
                    in_=w1_i[:, half, hc, ph * 512:(ph + 1) * 512]), LS[bi], 16)
            sc.wait("scalar", LS[bi], m_wd[1])
            m_wq, m_ws = quant_block(wsl, WQT[bi], 0, 1, wqt_free[bi], bi)
            w1_free[bi] = m_ws
            return m_wq

        sc.wait("scalar", PE, PE.n)
        sc.wait("scalar", DV, DV.n)
        m_wq0 = ffn1_quant(0, 0)

        m_x1 = layer_norm(rbf, sqb, x1b)
        sc.wait("sync", PE, PE.n)
        sc.wait("gpsimd", PE, PE.n)

        for half in range(2):
            m_wq_cur = m_wq0 if half == 0 else ffn1_quant(half, 0)
            for hc in range(32):
                bi = hc % 2
                hcG = half * 32 + hc
                if half == 0 and hc == 0:
                    sc.wait("tensor", DV, m_x1[1])
                sc.wait("tensor", *m_wq_cur)
                for th in range(2):
                    if m_hev.get((bi, th)) is not None:
                        sc.wait("tensor", *m_hev[(bi, th)])
                m_h = [None, None]
                for ic in range(IC):
                    for th in range(2):
                        p_ = 2 * bi + th
                        r = sc.emit("tensor", lambda e, bi=bi, ic=ic, th=th, p_=p_: e.matmul(
                            pb[p_][:], WQT[bi][:, ic * 128:(ic + 1) * 128],
                            x1b[:, ic * 1024 + th * 512: ic * 1024 + th * 512 + 512],
                            start=(ic == 0), stop=(ic == IC - 1)),
                            PE if ic == IC - 1 else None)
                        if r is not None:
                            m_h[th] = r
                wqt_free[bi] = (PE, PE.n)
                if hc + 1 < 32:
                    m_wq_cur = ffn1_quant(half, hc + 1)
                for th in range(2):
                    p_ = 2 * bi + th
                    sc.wait("scalar", PE, m_h[th][1])
                    m_g = sc.emit("scalar", lambda e, hc=hc, th=th, hcG=hcG, p_=p_: e.activation(
                        out=h_all[:, hc * 1024 + th * 512: hc * 1024 + th * 512 + 512],
                        in_=pb[p_][:], func=AF.Gelu,
                        bias=B1S[:, hcG:hcG + 1], scale=0.5), AC)
                    m_hev[(bi, th)] = (AC, m_g[1])
            if debug_taps and half == 0:
                sc.wait("sync", AC, AC.n)
                sc.emit("sync", lambda e: e.dma_start(out=dbg_h, in_=h_all), ST, 16)
            if half == 0:
                sc.wait("vector", DV, m_x1[1])
                for oc in range(IC):
                    m_accinit = sc.emit("vector", lambda e, oc=oc: e.tensor_scalar(
                        out=R2[:, oc * 1024:(oc + 1) * 1024],
                        in0=x1b[:, oc * 1024:(oc + 1) * 1024],
                        scalar1=B2S[:, oc:oc + 1], scalar2=None, op0=ALU.add), DV)
            sc.wait("tensor", AC, AC.n)
            sc.wait("vector", DV, DV.n)
            w2_free = list(w1_free)
            m_f2evac = [None, None]

            def ffn2_quant(half, oc, hs):
                bi = hs
                q = "sync" if bi == 0 else "gpsimd"
                wsl = WS32[bi]
                if w2_free[bi] is not None:
                    sc.wait(q, *w2_free[bi])
                for ph in range(4):
                    m_wd = sc.emit(q, lambda e, half=half, oc=oc, hs=hs, dst=wsl, ph=ph: e.dma_start(
                        out=dst[:, ph * 512:(ph + 1) * 512],
                        in_=w2_i[:, half, oc, hs, ph * 512:(ph + 1) * 512]), LS[bi], 16)
                sc.wait("scalar", LS[bi], m_wd[1])
                mq, ms = quant_block(wsl, WQT[bi], 2, 3, wqt_free[bi], bi)
                w2_free[bi] = ms
                wqt_free[bi] = None
                return mq

            m_wq2 = [ffn2_quant(half, 0, 0), ffn2_quant(half, 0, 1)]
            for oc in range(IC):
                for th in range(2):
                    if m_f2evac[th] is not None:
                        sc.wait("tensor", *m_f2evac[th])
                m_f = [None, None]
                m_wq2_next = [None, None]
                for hs in range(2):
                    sc.wait("tensor", *m_wq2[hs])
                    for icc in range(16):
                        hcc = hs * 16 + icc
                        for th in range(2):
                            r = sc.emit("tensor", lambda e, hs=hs, icc=icc, th=th, hcc=hcc: e.matmul(
                                pb[4 + th][:], WQT[hs][:, icc * 128:(icc + 1) * 128],
                                h_all[:, hcc * 1024 + th * 512: hcc * 1024 + th * 512 + 512],
                                start=(hcc == 0), stop=(hcc == 31)),
                                PE if icc == 15 else None)
                            if r is not None and hcc == 31:
                                m_f[th] = r
                    wqt_free[hs] = (PE, PE.n)
                    if oc + 1 < IC:
                        m_wq2_next[hs] = ffn2_quant(half, oc + 1, hs)
                for th in range(2):
                    sc.wait("vector", PE, m_f[th][1])
                    if half == 0 and oc == 0 and th == 0:
                        sc.wait("vector", DV, m_accinit[1])
                    col = oc * 1024 + th * 512
                    m_a2 = sc.emit("vector", lambda e, th=th, col=col: e.scalar_tensor_tensor(
                        out=R2[:, col:col + 512], in0=pb[4 + th][:], scalar=0.5,
                        in1=R2[:, col:col + 512], op0=ALU.mult, op1=ALU.add), DV)
                    m_f2evac[th] = (DV, m_a2[1])
                if oc + 1 < IC:
                    m_wq2 = m_wq2_next
            w1_free = list(w2_free)
            if debug_taps and half == 0:
                sc.wait("sync", DV, DV.n)
                sc.emit("sync", lambda e: e.dma_start(out=dbg_acc, in_=R2[:]), ST, 16)

        layer_norm(rbf2, sqb2, None, store_out=True)
        sc.wait("sync", ST, ST.n)

        def run_ops(e, eng):
            for op in sc.ops[eng]:
                if op[0] == "w":
                    _, s, v = op
                    e.wait_ge(s.h, v)
                else:
                    _, fn, inc, k = op
                    inst = fn(e)
                    if inc is not None:
                        inst.then_inc(inc.h, k)

        @block.sync
        def _(e):
            run_ops(e, "sync")

        @block.gpsimd
        def _(e):
            run_ops(e, "gpsimd")

        @block.scalar
        def _(e):
            run_ops(e, "scalar")

        @block.vector
        def _(e):
            run_ops(e, "vector")

        @block.tensor
        def _(e):
            run_ops(e, "tensor")

    return nc


def _rope_tables():
    theta = 1.0 / ROPE_BASE ** (np.arange(0, HD, 2, dtype=np.float32) / HD)
    t = np.arange(T, dtype=np.float32)
    ang = np.outer(theta.astype(np.float64), t.astype(np.float64))
    cos = np.cos(ang).astype(np.float32)
    sin = np.sin(ang).astype(np.float32)
    cosE = np.repeat(cos, 2, axis=0)
    sinS = np.empty((HD, T), np.float32)
    sinS[0::2] = -sin
    sinS[1::2] = sin
    return cosE, sinS


def prepare_in_maps(L0_attn_x, L0_attn_w_atten, L0_attn_w_proj, L0_ffn_w1,
                    L0_ffn_b1, L0_ffn_w2, L0_ffn_b2):
    x = np.ascontiguousarray(np.asarray(L0_attn_x, dtype=np.float32))
    wat = np.ascontiguousarray(np.asarray(L0_attn_w_atten, dtype=np.float32))
    wp = np.ascontiguousarray(np.asarray(L0_attn_w_proj, dtype=np.float32))
    w1 = np.ascontiguousarray(np.asarray(L0_ffn_w1, dtype=np.float32))
    b1 = np.asarray(L0_ffn_b1, dtype=np.float32)
    w2 = np.ascontiguousarray(np.asarray(L0_ffn_w2, dtype=np.float32))
    b2 = np.asarray(L0_ffn_b2, dtype=np.float32)

    bf = ml_dtypes.bfloat16
    x_fm = np.ascontiguousarray(
        x.transpose(2, 0, 1).reshape(IC, 128, B * T).transpose(1, 0, 2))
    wp_b = np.ascontiguousarray(
        wp.reshape(IC, 128, IC, 128).transpose(3, 0, 2, 1)
        .reshape(128, IC, 2048))
    w1_b = np.ascontiguousarray(
        w1.reshape(2, 32, 128, IC, 128).transpose(4, 0, 1, 3, 2)
        .reshape(128, 2, 32, 2048))
    w2_b = np.ascontiguousarray(
        w2.reshape(IC, 128, 2, 2, 16, 128).transpose(5, 2, 0, 3, 4, 1)
        .reshape(128, 2, IC, 2, 2048))
    b1s = np.ascontiguousarray(b1.reshape(64, 128).T)
    b2s = np.ascontiguousarray(b2.reshape(16, 128).T)
    cosE, sinS = _rope_tables()
    qc = cosE.astype(bf)
    qs = sinS.astype(bf)
    kc = cosE.astype(bf)
    ks = sinS.astype(bf)
    mask = np.zeros((128, 4, 512), np.float32)
    ktv = np.arange(128)[:, None]
    qtv = np.arange(512)[None, :]
    for p in range(4):
        mask[:, p, :] = np.where(qtv >= p * 128 + ktv, 0.0, -1e9)
    mask = np.ascontiguousarray(mask.reshape(128, 2048)).astype(bf)
    permm = np.zeros((128, 128), bf)
    for i in range(128):
        permm[i, i ^ 1] = 1
    idn = np.eye(128, dtype=np.float32).astype(bf)
    onesc = np.ones((1, 128), bf)
    onesk = np.ones((128, 1), bf)
    onesc32 = np.ones((1, 128), np.float32)
    onesk32 = np.ones((128, 1), np.float32)
    sel8 = np.zeros((8, 2), np.float32)
    sel8[0:4, 0] = 1.0
    sel8[4:8, 1] = 1.0

    def _sc_block(g):
        if g < 64:
            return w1_b[:, g // 32, g % 32, :]
        g2 = g - 64
        return w2_b[:, g2 // 32, (g2 % 32) // 2, g2 % 2, :]

    wsc_cores = [np.ascontiguousarray(np.stack(
        [_sc_block(16 * c + i) for i in range(16)], axis=1)) for c in range(NC)]

    shared = {
        "x_fm": x_fm, "wp_i": wp_b, "w1_i": w1_b, "w2_i": w2_b,
        "b1s_i": b1s, "b2s_i": b2s, "qc_i": qc, "qs_i": qs, "kc_i": kc,
        "ks_i": ks, "mask_i": mask, "perm_i": permm, "idn_i": idn,
        "onesc_i": onesc,
        "onesk_i": onesk, "onesc32_i": onesc32, "onesk32_i": onesk32,
        "sel8_i": sel8,
    }
    in_maps = []
    for c in range(NC):
        def _qkv(base):
            return np.ascontiguousarray(
                wat[base + (2 * c) * 128: base + (2 * c + 2) * 128, :]
                .reshape(2, 128, IC, 128).transpose(3, 2, 0, 1)
                .reshape(128, 4096))
        wq_c = _qkv(0)
        wk_c = _qkv(D)
        wv_c = _qkv(2 * D)
        x_lc = np.ascontiguousarray(x_fm[:, :, c * TOKL:(c + 1) * TOKL])
        in_maps.append(dict(shared, wq_i=wq_c, wk_i=wk_c, wv_i=wv_c,
                            x_loc=x_lc, wsc_i=wsc_cores[c]))
    return in_maps


def assemble(blks):
    out = np.empty((B, T, D), np.float32)
    for c in range(NC):
        blk = np.asarray(blks[c], dtype=np.float32)
        gt0 = c * TOKL
        out[gt0 // T, gt0 % T: gt0 % T + TOKL, :] = blk.T
    return out


def kernel(L0_attn_x, L0_attn_w_atten, L0_attn_w_proj, L0_ffn_w1, L0_ffn_b1,
           L0_ffn_w2, L0_ffn_b2):
    in_maps = prepare_in_maps(L0_attn_x, L0_attn_w_atten, L0_attn_w_proj,
                              L0_ffn_w1, L0_ffn_b1, L0_ffn_w2, L0_ffn_b2)
    nc = build_nc()
    res = run_bass_kernel_spmd(nc, in_maps, core_ids=list(range(NC)))
    return assemble([res.results[c]["out_fm"] for c in range(NC)])
